# version 53 (final)
# baseline (speedup 1.0000x reference)
import contextlib
import numpy as np
import ml_dtypes
import concourse.bass as bass
import concourse.mybir as mybir
from concourse.bass_utils import run_bass_kernel_spmd

F32 = mybir.dt.float32
BF16 = mybir.dt.bfloat16
AF = mybir.ActivationFunctionType
ALU = mybir.AluOpType
AX = mybir.AxisListType.X
MAGIC = float(np.float32(3 * 2.0 ** 22))
QMAX = 127.0
KB = 1024


class Cfg:
    def __init__(self, n_cores=8, T=2048, D=1024, I=2816):
        self.n_cores, self.T, self.D, self.I = n_cores, T, D, I
        assert T % 512 == 0 and D % 128 == 0 and I % 128 == 0
        self.NT, self.ND, self.NI, self.NTC = T // 128, D // 128, I // 128, T // 512


REAL = Cfg()


class PSem:
    def __init__(self, name):
        self.name, self.c, self.h = name, 0, None


def build_program(cfg: Cfg):
    nc = bass.Bass("TRN2", target_bir_lowering=False)
    T, D, I = cfg.T, cfg.D, cfg.I
    NT, ND, NI, NTC = cfg.NT, cfg.ND, cfg.NI, cfg.NTC
    NVH = (D + 511) // 512
    TH = T // 2

    d_x = nc.dram_tensor("x", [T, D], F32, kind="ExternalInput")
    d_wq = nc.dram_tensor("wq", [D, D], F32, kind="ExternalInput")
    d_wk = nc.dram_tensor("wk", [D, D], F32, kind="ExternalInput")
    d_wv = nc.dram_tensor("wv", [D, D], F32, kind="ExternalInput")
    d_wo = nc.dram_tensor("wo", [D, D], F32, kind="ExternalInput")
    d_wfbd = nc.dram_tensor("wfbd", [128, 128], F32, kind="ExternalInput")
    d_bf = nc.dram_tensor("bfr", [128, 1], F32, kind="ExternalInput")
    d_bo = nc.dram_tensor("bor", [128, D], F32, kind="ExternalInput")
    d_mask = nc.dram_tensor("maskbd", [128, 128], F32, kind="ExternalInput")
    d_ident = nc.dram_tensor("ident", [128, 128], BF16, kind="ExternalInput")
    d_ones = nc.dram_tensor("onescol", [128, 1], BF16, kind="ExternalInput")
    d_clip = nc.dram_tensor("clip2", [2], F32, kind="ExternalInput")
    d_wg = nc.dram_tensor("wg", [I, D], F32, kind="ExternalInput")
    d_wu = nc.dram_tensor("wu", [I, D], F32, kind="ExternalInput")
    d_wd = nc.dram_tensor("wd", [D, I], F32, kind="ExternalInput")
    d_y = nc.dram_tensor("y", [T, D], F32, kind="ExternalOutput")
    d_ya = nc.dram_tensor("ya_i", [T, D], BF16)
    d_wdsp = nc.dram_tensor("wdspill", [128, NI * D], BF16)
    d_cc1i = nc.dram_tensor("cc1i", [1], F32)
    d_cc1o = nc.dram_tensor("cc1o", [1], F32)
    d_cc2i = nc.dram_tensor("cc2i", [128, ND * 129], F32)
    d_cc2o = nc.dram_tensor("cc2o", [128, ND * 129], F32)
    d_cc3i = nc.dram_tensor("cc3i", [1], F32)
    d_cc3o = nc.dram_tensor("cc3o", [1], F32)
    d_identf = nc.dram_tensor("identf", [128, 128], F32, kind="ExternalInput")
    d_bc1 = nc.dram_tensor("bc1", [2], F32)
    d_bc3 = nc.dram_tensor("bc3", [2], F32)

    sems = [PSem(n) for n in (
        "st", "sv", "ss", "sgp", "dt", "scc", "dsc",
        "dsx0", "dsx1", "dsx2", "dsx3", "dsw0", "dsw1", "dsp",
        "dgv", "dgk", "dgq", "dgo", "dsya", "dsyb", "dwr",
        "dswg0", "dswg1", "dswu0", "dswu1", "dsra", "dsrb", "dsy2a", "dsy2b",
        "dbx", "dbh", "dbs")]
    (st, sv, ss, sgp, dt, scc, dsc, dsx0, dsx1, dsx2, dsx3, dsw0, dsw1, dsp,
     dgv, dgk, dgq, dgo, dsya, dsyb, dwr, dswg0, dswg1, dswu0, dswu1,
     dsra, dsrb, dsy2a, dsy2b, dbx, dbh, dbs) = sems
    dswg = [dswg0, dswg1]
    dswu = [dswu0, dswu1]
    dsx = [dsx0, dsx1, dsx2, dsx3]
    dsw = [dsw0, dsw1]
    dsy = [dsya, dsyb]
    dsr = [dsra, dsrb]
    dsy2 = [dsy2a, dsy2b]
    queues = {e: [] for e in ("tensor", "vector", "scalar", "sync", "gpsimd")}

    def E(eng, fn, waits=(), inc=None, n=1):
        waits = tuple((s, v) for (s, v) in waits if s is not None and v > 0)
        if inc is not None:
            inc.c += n

        def do(e, waits=waits, fn=fn, inc=inc, n=n):
            for s, v in waits:
                e.wait_ge(s.h, v)
            ins = fn(e)
            if inc is not None:
                ins.then_inc(inc.h, n)
        queues[eng].append(do)
        return (inc, inc.c) if inc is not None else None

    def wm(s):
        return (s, s.c)

    stack = contextlib.ExitStack()
    BASE0 = 16512
    assert BASE0 >= nc.sbuf_base

    def sb(name, shape, dtype, lo, hi=None):
        nbytes = shape[1] * (2 if dtype == BF16 else 4)
        if hi is None:
            hi = lo + nbytes
        assert lo % 32 == 0 and hi - lo >= nbytes, (name, lo, hi, nbytes)
        assert BASE0 + hi <= nc.sbuf_top, (name, lo, hi)
        return nc.alloc_sbuf_tensor_at(name, shape, dtype, offset=BASE0 + lo)

    IH = I // 2
    assert (I // 2) % 128 == 0 or I * 4 <= 11 * KB or True
    o_wnat0 = 6 * KB
    o_wnat1 = o_wnat0 + 4 * IH
    o_tqd0 = o_wnat1 + 4 * IH
    o_tqd1 = o_tqd0 + 2048
    o_wqdn = o_tqd1 + 2048
    o_xnst = max(o_wqdn, 6 * KB + 16 * D)
    o_wv = max(o_wqdn + 2 * I, o_xnst + 8 * D)
    o_wv += (-o_wv) % 32
    o_wk = o_wv + 2 * ND * D
    o_vtm = o_wk + 2 * ND * D
    assert o_vtm + 2 * NT * D <= 94 * KB, (o_vtm, "A2a-era overflow")
    o_qf = 94 * KB - 2 * ND * T
    o_ofm = o_qf - 2 * ND * T
    assert o_qf >= o_vtm or o_qf >= o_wk + 2 * ND * D, "qf overlaps live wq"
    assert o_ofm >= o_wv
    o_xrb = 6 * KB
    o_rn = o_xrb + 16 * D
    assert o_rn + 4 * T <= o_ofm

    assert NT * D * 4 <= 64 * KB
    assert ND * T * 2 <= 32 * KB
    assert NI * T * 2 <= 88 * KB
    assert NI * D * 2 <= 44 * KB
    assert ND * D * 2 <= 16 * KB
    assert ND * 129 * 4 <= 4200

    c_ident = sb("ident_s", [128, 128], BF16, 0)
    c_wfbd = sb("wfbd_s", [128, 128], BF16, 256)
    c_mask = sb("mask_s", [128, 128], F32, 512)
    c_ones = sb("ones_s", [128, 1], BF16, 1024, 1088)
    c_bf = sb("bf_s", [128, 1], F32, 1088, 1152)
    c_kvbd = sb("kvbd_s", [128, 128], BF16, 1152)
    c_ksr = sb("ksr_s", [128, 128], BF16, 1408)
    c_wfbdf = sb("wfbdf_s", [128, 128], F32, 1664)
    c_cols = sb("cols_s", [128, 96], F32, 2176)
    c_stats = sb("stats_s", [128, 48], F32, 2560)
    c_clip = sb("clip_s", [1, 2], F32, 2752, 2784)
    c_sig = sb("sig_s", [1, 2], F32, 2784, 2816)
    c_s1 = sb("s1_s", [1, 8], F32, 2816, 2880)
    c_bo = sb("bo_s", [128, D], F32, 202 * KB + 512)
    c_identf = sb("identf_s", [128, 128], F32, 2880)
    assert 2880 + 512 <= 6 * KB

    c_xr = sb("xr", [128, 4 * D], F32, 6 * KB)
    c_hpre = sb("hpre", [128, NI * T], BF16, 6 * KB)
    c_wdst0 = sb("wdst0", [128, IH], F32, o_wnat0)
    c_wdst1 = sb("wdst1", [128, IH], F32, o_wnat1)
    c_tqd0 = sb("tqd0", [128, 512], F32, o_tqd0)
    c_tqd1 = sb("tqd1", [128, 512], F32, o_tqd1)
    c_wqdn = sb("wqdn", [128, I], BF16, o_wqdn)
    c_wv = sb("wv_s", [128, ND * D], BF16, o_wv)
    c_wk = sb("wk_s", [128, ND * D], BF16, o_wk)
    c_wq = sb("wq_s", [128, ND * D], BF16, o_wk)
    c_vtm = sb("vtm", [128, NT * D], BF16, o_vtm)
    c_xnst = [sb(f"xnst{k}", [128, D], BF16, o_xnst + k * 2 * D) for k in range(4)]
    c_qf = sb("qf", [128, ND * T], BF16, o_qf)
    o_sld = o_vtm
    c_wgs = [sb(f"wg{k}s", [128, D], F32, o_sld + k * 4 * D) for k in range(2)]
    c_wus = [sb(f"wu{k}s", [128, D], F32, o_sld + (2 + k) * 4 * D) for k in range(2)]
    c_tqw01 = sb("tqw01", [128, D], F32, o_sld + 16 * D)
    c_ixst2 = sb("ixst2", [128, 512], F32, o_sld + 20 * D)
    c_wqng01 = [sb(f"wqng01_{k}", [128, D], BF16,
                   o_sld + 20 * D + 2048 + k * 4 * D) for k in range(2)]
    c_wqnu01 = [sb(f"wqnu01_{k}", [128, D], BF16,
                   o_sld + 20 * D + 2048 + 2 * D + k * 4 * D) for k in range(2)]
    assert o_sld + 20 * D + 2048 + 8 * D <= 94 * KB
    c_rn2 = sb("rn2", [128, T], F32, 6 * KB)
    assert 6 * KB + 4 * T <= o_xnst
    c_ofm = sb("ofm", [128, ND * T], BF16, o_ofm)
    c_xrb = sb("xrb", [128, 4 * D], F32, o_xrb)
    c_rn = sb("rn", [128, T], F32, o_rn)
    c_xn = sb("xn", [128, ND * T], BF16, 94 * KB)
    c_intx = sb("intx", [128, ND * T], BF16, 94 * KB)
    c_ytile = sb("ytile", [128, 2 * D], F32, 94 * KB)
    c_accrb = sb("accrb", [128, 2 * D], F32, 94 * KB + 8 * D)
    c_ixst = sb("ixst", [128, T], F32, 126 * KB)
    c_yst0 = sb("yst0", [128, D], BF16, 126 * KB)
    c_yst1 = sb("yst1", [128, D], BF16, 126 * KB + 4 * D)
    c_yab = sb("yab", [128, 2 * D], BF16, 94 * KB + 16 * D)
    c_tq = sb("tq", [128, 2048], F32, 126 * KB)
    c_tqw = sb("tqw", [128, 1024], F32, 126 * KB)
    c_wqng = sb("wqng", [128, D], BF16, 130 * KB)
    c_wqnu = sb("wqnu", [128, D], BF16, 132 * KB)
    assert 132 * KB + 2 * D <= 134 * KB
    c_wdqT = sb("wdqT", [128, NI * D], BF16, 134 * KB)
    c_sta = sb("sta", [128, T], BF16, 178 * KB)
    c_stb = sb("stb", [128, T], BF16, 178 * KB + 2 * T)
    c_stc = sb("stc", [128, T], BF16, 178 * KB + 4 * T)
    assert 178 * KB + 6 * T <= 194 * KB
    c_wo = sb("wo_s", [128, ND * D], BF16, 178 * KB)
    c_wgn = [sb("wgn0", [128, D], F32, 178 * KB), sb("wgn1", [128, D], F32, 182 * KB)]
    c_wun = [sb("wun0", [128, D], F32, 186 * KB), sb("wun1", [128, D], F32, 190 * KB)]
    c_kvs = sb("kvs", [128, ND * 129], F32, 194 * KB)
    c_kvr = sb("kvr", [128, ND * 129], F32, 198 * KB + 256)
    c_sg = sb("sg", [128, T], F32, 194 * KB)
    c_slabg = sb("slabg", [128, D], BF16, 202 * KB + 512)
    c_slabu = sb("slabu", [128, D], BF16, 202 * KB + 512 + 2 * D)
    assert 202 * KB + 512 + 4 * D <= 207 * KB

    c_ps = stack.enter_context(nc.psum_tensor("ps", [128, 4096], F32))

    COL = {}

    def col(name):
        if name not in COL:
            COL[name] = len(COL)
            assert len(COL) <= 96
        j = COL[name]
        return c_cols[:, j:j + 1]

    RG = [list(range(cfg.n_cores))]

    E("sync", lambda e: e.dma_start(out=c_ident[:], in_=d_ident[:, :]), inc=dsc, n=16)
    E("sync", lambda e: e.dma_start(out=c_wfbdf[:], in_=d_wfbd[:, :]), inc=dsc, n=16)
    E("sync", lambda e: e.dma_start(out=c_mask[:], in_=d_mask[:, :]), inc=dsc, n=16)
    E("sync", lambda e: e.dma_start(out=c_ones[:], in_=d_ones[:, :]), inc=dsc, n=16)
    E("sync", lambda e: e.dma_start(out=c_bf[:], in_=d_bf[:, :]), inc=dsc, n=16)
    E("sync", lambda e: e.dma_start(out=c_bo[:], in_=d_bo[:, :]), inc=dsc, n=16)
    E("sync", lambda e: e.dma_start(out=c_clip[:, :], in_=d_clip[None, :]), inc=dsc, n=16)
    E("sync", lambda e: e.dma_start(out=c_identf[:], in_=d_identf[:, :]), inc=dsc, n=16)
    ev_consts = wm(dsc)
    ev_ident = ev_mask = ev_ones = ev_bfl = ev_bo = ev_clip = ev_consts
    sigxb, sigdb = col("sigxb"), col("sigdb")
    assert COL["sigdb"] == COL["sigxb"] + 1
    ev_wvld_c = []
    _wv_sems = [dgv, dsp, dwr]
    _wv_prev = [None] * len(_wv_sems)
    for c in range(ND):
        k = c % len(_wv_sems)
        pv = _wv_prev[k]
        ev = E("gpsimd", lambda e, c=c, k=k: e.dma_start(
            out=c_wv[:, c * D:(c + 1) * D], in_=d_wv[c * 128:(c + 1) * 128, :]),
            waits=(pv,) if pv is not None else (), inc=_wv_sems[k], n=16)
        _wv_prev[k] = ev
        ev_wvld_c.append(ev)
    ev_wvtail = [e for e in _wv_prev if e is not None]

    nhalf = (D + 511) // 512
    ev_xn = [None] * NT
    ev_xn_half = [None] * NT
    ev_xnev_a = [None, None]
    ev_vtm = [None] * NT
    ev_tpose = [None] * 4
    ev_xnev = [None, None]
    ev_vev = [None, None]
    ev_apply_slot = [None] * 4
    ev_applyt = [None] * NT
    ev_mxacc = None
    mxa = col("mxacc")

    def emit_LN(tt):
        par = tt % 2
        slot = tt % 4
        xrs = c_xr[:, slot * D:(slot + 1) * D]
        w_x = [ev_apply_slot[slot]] if ev_apply_slot[slot] is not None else []
        ev_xd = E("sync", lambda e, tt=tt, xrs=xrs: e.dma_start(
            out=xrs, in_=d_x[tt * 128:(tt + 1) * 128, :]),
            waits=tuple(w_x), inc=dsx[slot], n=16)
        ev_xd_t[tt] = ev_xd
        stats = c_stats[:, par * 24: par * 24 + 6 * nhalf]
        agg2 = c_stats[:, 36 + par * 2: 38 + par * 2]
        ev_bn = None
        for h in range(nhalf):
            lo, hi = h * 512, min(D, (h + 1) * 512)
            ev_bn = E("vector", lambda e, h=h, lo=lo, hi=hi, stats=stats, xrs=xrs: e.bn_stats(
                stats[:, 6 * h:6 * h + 6], xrs[:, lo:hi]),
                waits=(ev_xd,) if h == 0 else (), inc=(sv if h == nhalf - 1 else None))
        ev_agg = E("vector", lambda e, stats=stats, agg2=agg2: e.bn_aggr(agg2, stats),
                   waits=(ev_bn,), inc=sv)
        veps, s0 = col(f"veps{par}"), col(f"s0{par}")
        rstd, nbias = col(f"rstd{par}"), col(f"nbias{par}")
        ev_veps = E("vector", lambda e, agg2=agg2, veps=veps: e.tensor_scalar(
            veps, agg2[:, 1:2], 1e-5, None, ALU.add), waits=(ev_agg,), inc=sv)
        ev_s0 = E("scalar", lambda e, veps=veps, s0=s0: e.activation(s0, veps, AF.Sqrt),
                  waits=(ev_veps,), inc=ss)
        ev_rstd = E("vector", lambda e, s0=s0, rstd=rstd: e.reciprocal(rstd, s0),
                    waits=(ev_s0,), inc=sv)
        ev_nb = E("vector", lambda e, agg2=agg2, rstd=rstd, nbias=nbias: e.tensor_scalar(
            nbias, agg2[:, 0:1], -1.0, rstd, ALU.mult, ALU.mult), waits=(ev_rstd,), inc=sv)
        xnst = c_xnst[slot]
        w_ap = [ev_nb] + ([ev_tpose[slot]] if ev_tpose[slot] is not None else [])
        ev_apply = E("vector", lambda e, xnst=xnst, rstd=rstd, nbias=nbias, xrs=xrs:
                     e.tensor_scalar(xnst[:], xrs, rstd, nbias, ALU.mult, ALU.add),
                     waits=tuple(w_ap), inc=sv)
        ev_apply_slot[slot] = ev_apply
        ev_applyt[tt] = ev_apply

    def emit_PE_A(tt):
        par = tt % 2
        slot = tt % 4
        xnst = c_xnst[slot]
        pb = 0 if par == 0 else 2
        w_t = [ev_applyt[tt], ev_ident]
        if ev_xnev[par] is not None:
            w_t += [ev_xnev[par], ev_xnev_a[par]]
        ev_tp = None
        for c in range(ND):
            bslot, off = pb + c // 4, (c % 4) * 128
            ev_tp = E("tensor", lambda e, xnst=xnst, c=c, bslot=bslot, off=off: e.matmul(
                c_ps[:, bslot * 512 + off: bslot * 512 + off + 128],
                xnst[:, c * 128:(c + 1) * 128], c_ident[:], start=True, stop=True),
                waits=tuple(w_t) if c == 0 else (), inc=(st if c == ND - 1 else None))
        ev_tpose[slot] = ev_tp
        ev_xn[tt] = E("scalar", lambda e, tt=tt, pb=pb: e.copy(
            c_xn[:, tt * D: tt * D + D], c_ps[:, pb * 512: pb * 512 + D]),
            waits=(ev_tp,), inc=ss)
        ev_xn_half[tt] = ev_xn[tt]
        ev_xnev_a[par] = ev_xn[tt]
        ev_xnev[par] = ev_xn[tt]

    def emit_PE_B(tt):
        par = tt % 2
        b0v = 4 if par == 0 else 6
        w0 = [ev_xn[tt]] + ([ev_wvld_c[0]] if tt < 6 else ev_wvtail)
        if ev_vev[par] is not None:
            w0.append(ev_vev[par])
        ev_vmm = None
        for c in range(ND):
            for h in range(NVH):
                lo, hi = h * 512, min(D, (h + 1) * 512)
                w_c = ()
                if c == 0 and h == 0:
                    w_c = tuple(w0)
                elif tt < 6 and h == 0:
                    w_c = (ev_wvld_c[c],)
                ev_vmm = E("tensor", lambda e, tt=tt, c=c, lo=lo, hi=hi, b0v=b0v: e.matmul(
                    c_ps[:, b0v * 512 + lo: b0v * 512 + hi],
                    c_xn[:, tt * D + c * 128: tt * D + (c + 1) * 128],
                    c_wv[:, c * D + lo: c * D + hi],
                    start=(c == 0), stop=(c == ND - 1)),
                    waits=w_c,
                    inc=(st if (c == ND - 1 and h == NVH - 1) else None))
        ev_vtm[tt] = E("vector", lambda e, tt=tt, b0v=b0v: e.tensor_copy(
            c_vtm[:, tt * D:(tt + 1) * D], c_ps[:, b0v * 512: b0v * 512 + D]),
            waits=(ev_vmm,), inc=sv)
        ev_vev[par] = ev_vtm[tt]

    mxh = col("mxh")
    mxh2 = col("mxh2")
    ev_mh1 = None
    ev_mh2 = None
    ev_wkld = None
    ev_xd_t = {}
    for tt in range(NT):
        if tt >= 1:
            emit_PE_A(tt - 1)
        emit_LN(tt)
        if tt == min(9, NT - 1):
            for c in range(ND):
                ev_wkld = E("gpsimd", lambda e, c=c: e.dma_start(
                    out=c_wk[:, c * D:(c + 1) * D], in_=d_wk[c * 128:(c + 1) * 128, :]),
                    waits=(ev_wkld,) if ev_wkld is not None else (ev_applyt[tt - 1],),
                    inc=dgk, n=16)
        if tt == NT // 2 + 1:
            ev_mh1 = E("vector", lambda e: e.tensor_reduce(
                mxh, c_xn[:, 0:(NT // 2) * D], axis=AX, op=ALU.max,
                apply_absolute_value=True),
                waits=(ev_xn[NT // 2 - 1], ev_xn_half[NT // 2 - 1]), inc=sv)
        if tt == (3 * NT) // 4 + 1 and NT >= 8:
            ev_mh2 = E("vector", lambda e: e.tensor_reduce(
                mxh2, c_xn[:, (NT // 2) * D: (3 * NT // 4) * D], axis=AX,
                op=ALU.max, apply_absolute_value=True),
                waits=(ev_xn[(3 * NT) // 4 - 1], ev_xn_half[(3 * NT) // 4 - 1]),
                inc=sv)
        if tt >= 1:
            emit_PE_B(tt - 1)
    emit_PE_A(NT - 1)
    lo_f = ((3 * NT // 4) if NT >= 8 else (NT // 2)) * D
    ev_mxacc = E("vector", lambda e, lo_f=lo_f: e.tensor_reduce(
        mxa, c_xn[:, lo_f: NT * D], axis=AX, op=ALU.max,
        apply_absolute_value=True),
        waits=(ev_xn[NT - 1], ev_xn_half[NT - 1], ev_mh1), inc=sv)
    ev_mxacc = E("vector", lambda e: e.tensor_tensor(mxa, mxa, mxh, ALU.max),
                 waits=(ev_mxacc,), inc=sv)
    if NT >= 8:
        ev_mxacc = E("vector", lambda e: e.tensor_tensor(mxa, mxa, mxh2, ALU.max),
                     waits=(ev_mxacc, ev_mh2), inc=sv)
    emit_PE_B(NT - 1)
    ev_wfbd = E("vector", lambda e: e.tensor_copy(c_wfbd[:], c_wfbdf[:]),
                waits=(ev_consts,), inc=sv)
    ev_sig = E("scalar", lambda e: e.activation(c_sig[:, :], c_clip[:, :], AF.Sigmoid),
               waits=(ev_consts,), inc=ss)
    ev_s127 = E("vector", lambda e: e.tensor_scalar(
        c_s1[:, 2:4], c_sig[:, :], 1.0 / QMAX, None, ALU.mult), waits=(ev_sig,), inc=sv)
    ev_bsg = E("sync", lambda e: e.dma_start(out=d_bc1[None, :], in_=c_s1[:, 2:4]),
               waits=(ev_s127,), inc=dbs, n=16)
    E("sync", lambda e: e.dma_start(
        out=c_cols[:, COL["sigxb"]:COL["sigxb"] + 2],
        in_=bass.AP(d_bc1, 0, [[0, 128], [1, 2]])), waits=(ev_bsg,), inc=dbs, n=16)
    ev_sig_bc = wm(dbs)
    wm_sv_A = wm(sv)
    wm_ss_A = wm(ss)
    wm_st_A = wm(st)

    ev_r1mm = E("tensor", lambda e: e.matmul(
        c_ps[0:1, 0:128], mxa, c_identf[:], start=True, stop=True),
        waits=(ev_mxacc, ev_consts, wm_ss_A), inc=st)
    gmx1 = c_s1[:, 0:1]
    ev_ar1row = E("vector", lambda e: e.tensor_reduce(
        gmx1, c_ps[0:1, 0:128], axis=AX, op=ALU.max), waits=(ev_r1mm,), inc=sv)
    ev_c1w = E("sync", lambda e: e.dma_start(out=d_cc1i[None, :], in_=gmx1),
               waits=(ev_ar1row,), inc=dt, n=16)
    ev_ar1 = E("gpsimd", lambda e: e.collective_compute(
        "AllReduce", ALU.max, replica_groups=RG,
        ins=[d_cc1i.ap().opt()], outs=[d_cc1o.ap().opt()]), waits=(ev_c1w,), inc=scc)
    sxb, rsxb = col("sxb"), col("rsxb")
    gmxb = col("gmxb")
    ev_gmxb = E("gpsimd", lambda e: e.dma_start(
        out=gmxb, in_=bass.AP(d_cc1o, 0, [[0, 128], [1, 1]])),
        waits=(ev_ar1,), inc=dbx, n=16)

    NWSEG = (IH + 511) // 512
    ev_wd_ld = {}
    ev_wd_p1 = {}
    ev_wd_p2 = {}
    ev_tqd_free = [None, None]
    ev_wqdn_read = None
    ev_wd_evac = None

    def emit_wd_load(hc):
        if hc >= ND:
            return
        for half, wnat in ((0, c_wdst0), (1, c_wdst1)):
            w_ld = [wm_sv_A, wm_st_A] + [wm(dx) for dx in dsx]
            if (hc - 1, half) in ev_wd_p1:
                w_ld = [ev_wd_p1[(hc - 1, half)]]
            ev_wd_ld[(hc, half)] = E("sync", lambda e, hc=hc, half=half, wnat=wnat: e.dma_start(
                out=wnat[:], in_=d_wd[hc * 128:(hc + 1) * 128, half * IH:(half + 1) * IH]),
                waits=tuple(w_ld), inc=dsw[half], n=16)

    def emit_wd_quant(hc):
        wmx0, wmx1 = col("wdm0"), col("wdm1")
        r0, rsw, sw = col("wdr0"), col("wdrs"), col("wdsw")
        ev_m0 = E("vector", lambda e, wmx0=wmx0: e.tensor_reduce(
            wmx0, c_wdst0[:], axis=AX, op=ALU.max, apply_absolute_value=True),
            waits=(ev_wd_ld[(hc, 0)],), inc=sv)
        ev_m1 = E("vector", lambda e, wmx1=wmx1: e.tensor_reduce(
            wmx1, c_wdst1[:], axis=AX, op=ALU.max, apply_absolute_value=True),
            waits=(ev_wd_ld[(hc, 1)],), inc=sv)
        ev_mm_ = E("vector", lambda e, wmx0=wmx0, wmx1=wmx1: e.tensor_tensor(
            wmx0, wmx0, wmx1, ALU.max), waits=(ev_m0, ev_m1), inc=sv)
        ev_r0 = E("vector", lambda e, wmx0=wmx0, r0=r0: e.reciprocal(r0, wmx0),
                  waits=(ev_mm_,), inc=sv)
        ev_rsw = E("vector", lambda e, r0=r0, rsw=rsw: e.tensor_scalar(
            rsw, r0, QMAX, None, ALU.mult), waits=(ev_r0,), inc=sv)
        ev_sw = E("vector", lambda e, wmx0=wmx0, sw=sw: e.tensor_scalar(
            sw, wmx0, 1.0 / QMAX, None, ALU.mult), waits=(ev_rsw,), inc=sv)
        p2s = []
        ev_p1_last = [None, None]
        segi = 0
        for half, wnat in ((0, c_wdst0), (1, c_wdst1)):
            for sg_ in range(NWSEG):
                lo, hi = sg_ * 512, min(IH, (sg_ + 1) * 512)
                tqd = c_tqd0 if segi % 2 == 0 else c_tqd1
                w_p1 = [ev_sw]
                if ev_tqd_free[segi % 2] is not None:
                    w_p1.append(ev_tqd_free[segi % 2])
                ev_p1 = E("scalar", lambda e, wnat=wnat, lo=lo, hi=hi, tqd=tqd, rsw=rsw:
                          e.activation(tqd[:, 0:hi - lo], wnat[:, lo:hi],
                                       AF.Copy, bias=MAGIC, scale=rsw),
                          waits=tuple(w_p1), inc=ss)
                ev_p1_last[half] = ev_p1
                w_p2 = [ev_p1] + ([ev_wqdn_read] if ev_wqdn_read is not None else [])
                ev_p2 = E("vector", lambda e, half=half, lo=lo, hi=hi, tqd=tqd, sw=sw:
                          e.tensor_scalar(c_wqdn[:, half * IH + lo: half * IH + hi],
                                          tqd[:, 0:hi - lo], MAGIC, sw,
                                          ALU.subtract, ALU.mult),
                          waits=tuple(w_p2), inc=sv)
                ev_tqd_free[segi % 2] = ev_p2
                p2s.append(((half * IH + lo) // 128, (half * IH + hi + 127) // 128, ev_p2))
                segi += 1
        ev_wd_p1[(hc, 0)] = ev_p1_last[0]
        ev_wd_p1[(hc, 1)] = ev_p1_last[1]
        ev_wd_p2[hc] = p2s

    def p2_cover(hc, ic):
        best = None
        for lo_t, hi_t, ev in ev_wd_p2[hc]:
            if lo_t < ic + 1 and hi_t > ic:
                if best is None or ev[1] > best[1]:
                    best = ev
        return best if best is not None else ev_wd_p2[hc][-1][2]

    emit_wd_load(0)
    prev = None
    ev_kv_all = None
    ev_kmm_last = None
    for hp in range(ND):
        w_k = [ev_wkld]
        if prev is None:
            w_k += [ev_ar1row, wm_ss_A]
        else:
            w_k += [prev["tr_ev"], prev["ks_ev"]]
        ev_kmm = None
        for c in range(ND):
            for j in range(NTC):
                ev_kmm = E("tensor", lambda e, hp=hp, c=c, j=j: e.matmul(
                    c_ps[:, j * 512:(j + 1) * 512],
                    c_wk[:, c * D + hp * 128: c * D + hp * 128 + 128],
                    bass.AP(c_xn, (4 * j) * D + c * 128, [[ND * T, 128], [D, 4], [1, 128]]),
                    start=(c == 0), stop=(c == ND - 1)),
                    waits=tuple(w_k) if (c == 0 and j == 0) else (),
                    inc=(st if (c == ND - 1 and j == NTC - 1) else None))
        ev_kmm_last = ev_kmm
        w_kev = [ev_kmm] + ([prev["kf_mm"]] if prev is not None else [])
        ev_kev_a = E("scalar", lambda e: e.copy(c_sta[:, 0:T // 2], c_ps[:, 0:T // 2]),
                     waits=tuple(w_kev), inc=ss)
        ev_kev = E("scalar", lambda e: e.copy(c_sta[:, T // 2:T], c_ps[:, T // 2:T]),
                   waits=(), inc=ss)
        ev_kfmm = ev_kfmm_h1 = None
        for j in range(NTC):
            w_kf = [ev_kev_a if j < NTC // 2 else ev_kev, ev_wfbd]
            if j == 0:
                if prev is not None:
                    w_kf += [prev["ks_ev"], prev["wd_ev"]]
                else:
                    w_kf += [ev_vtm[NT - 1], ev_vtm[NT - 2] if NT >= 2 else ev_vtm[NT - 1]]
            ev_kfmm = E("tensor", lambda e, j=j: e.matmul(
                c_ps[:, (4 + j) * 512:(5 + j) * 512], c_wfbd[:],
                c_sta[:, j * 512:(j + 1) * 512], start=True, stop=True),
                waits=tuple(w_kf) if j in (0, NTC // 2) else (),
                inc=(st if j in (NTC // 2 - 1, NTC - 1) else None))
            if j == NTC // 2 - 1:
                ev_kfmm_h1 = ev_kfmm
        w_kfe = [ev_kfmm_h1, ev_bfl] + ([prev["tr_mm"]] if prev is not None else [])
        ev_kfev_a = E("scalar", lambda e: e.activation(
            c_stb[:, 0:T // 2], c_ps[:, 4 * 512: 4 * 512 + T // 2],
            AF.Relu, bias=c_bf[:], scale=1.0), waits=tuple(w_kfe), inc=ss)
        ev_kfev = E("scalar", lambda e: e.activation(
            c_stb[:, T // 2:T], c_ps[:, 4 * 512 + T // 2: 4 * 512 + T],
            AF.Relu, bias=c_bf[:], scale=1.0), waits=(ev_kfmm,), inc=ss)
        ev_trmm = ev_trmm_h1 = None
        for tt in range(NT):
            bslot, off = tt // 4, (tt % 4) * 128
            w_t = [ev_kfev_a if tt < NT // 2 else ev_kfev]
            if prev is not None and tt == 0:
                w_t.append(prev["tr_ev"])
            ev_trmm = E("tensor", lambda e, tt=tt, bslot=bslot, off=off: e.matmul(
                c_ps[:, bslot * 512 + off: bslot * 512 + off + 128],
                c_stb[:, tt * 128:(tt + 1) * 128], c_ident[:], start=True, stop=True),
                waits=tuple(w_t) if tt in (0, NT // 2) else (),
                inc=(st if tt in (NT // 2 - 1, NT - 1) else None))
            if tt == NT // 2 - 1:
                ev_trmm_h1 = ev_trmm
        w_tre = [ev_trmm_h1] + ([prev["kv_mm"]] if prev is not None else [])
        ev_trev_a = E("scalar", lambda e: e.copy(c_stc[:, 0:T // 2], c_ps[:, 0:T // 2]),
                      waits=tuple(w_tre), inc=ss)
        ev_trev = E("scalar", lambda e: e.copy(c_stc[:, T // 2:T], c_ps[:, T // 2:T]),
                    waits=(ev_trmm,), inc=ss)
        emit_wd_quant(hp)
        emit_wd_load(hp + 1)
        ev_kvmm = None
        for tt in range(NT):
            w_kv = [ev_trev_a, ev_ones, ev_vtm[tt]]
            if prev is not None and tt == 0:
                w_kv.append(prev["ks_ev"])
            if tt == NT // 2:
                w_kv = [ev_vtm[tt], ev_trev]
            E("tensor", lambda e, tt=tt, hp=hp: e.matmul(
                c_ps[:, 4 * 512: 4 * 512 + 128], c_stc[:, tt * 128:(tt + 1) * 128],
                c_vtm[:, tt * D + hp * 128: tt * D + hp * 128 + 128],
                start=(tt == 0), stop=(tt == NT - 1)),
                waits=tuple(w_kv) if (tt == 0 or tt == NT // 2) else (ev_vtm[tt],))
            ev_kvmm = E("tensor", lambda e, tt=tt: e.matmul(
                c_ps[:, 5 * 512: 5 * 512 + 1], c_stc[:, tt * 128:(tt + 1) * 128], c_ones[:],
                start=(tt == 0), stop=(tt == NT - 1)),
                inc=(st if tt == NT - 1 else None))
        ev_kvev = E("vector", lambda e, hp=hp: e.tensor_copy(
            c_kvs[:, hp * 129: hp * 129 + 128], c_ps[:, 4 * 512: 4 * 512 + 128]),
            waits=(ev_kvmm,), inc=sv)
        ev_ksev = E("vector", lambda e, hp=hp: e.tensor_copy(
            c_kvs[:, hp * 129 + 128: hp * 129 + 129], c_ps[:, 5 * 512: 5 * 512 + 1]),
            waits=(ev_kvev,), inc=sv)
        ngrp = (NI + 7) // 8
        ev_wd_ev_hp = None
        for g in range(ngrp):
            ic0, ic1 = g * 8, min(NI, g * 8 + 8)
            ev_tl = None
            for ic in range(ic0, ic1):
                slot_ = ic - ic0
                bslot, off = 6 + slot_ // 4, (slot_ % 4) * 128
                w_t = [p2_cover(hp, ic)]
                if ic == ic0:
                    w_t += [ev_ident, ev_kfev]
                    if g > 0:
                        w_t.append(ev_wd_ev_hp)
                    elif prev is not None:
                        w_t.append(prev["wd_ev"])
                ev_tl = E("tensor", lambda e, ic=ic, hp=hp, bslot=bslot, off=off: e.matmul(
                    c_ps[:, bslot * 512 + off: bslot * 512 + off + 128],
                    c_wqdn[:, ic * 128:(ic + 1) * 128], c_ident[:], start=True, stop=True),
                    waits=tuple(w_t),
                    inc=(st if ic == ic1 - 1 else None))
            cnt = ic1 - ic0

            def wdevac(e, ic0=ic0, cnt=cnt, hc=hp):
                src = c_ps[:, 6 * 512: 6 * 512 + cnt * 128]
                dst = bass.AP(c_wdqT, ic0 * D + hc * 128, [[NI * D, 128], [D, cnt], [1, 128]])
                return e.copy(dst, src)
            ev_wd_ev_hp = E("scalar", lambda e, f=wdevac: f(e), waits=(ev_tl,), inc=ss)
        ev_wqdn_read = ev_tl if False else ev_tl
        ev_wd_evac = ev_wd_ev_hp
        prev = {"tr_ev": ev_trev, "tr_mm": ev_trmm, "ks_ev": ev_ksev, "kf_ev": ev_kfev,
                "kf_mm": ev_kfmm, "kv_mm": ev_kvmm, "wd_ev": ev_wd_ev_hp}
        ev_kv_all = ev_ksev
    ev_wd_done = ev_wd_evac

    ev_wqld = None
    for c in range(ND):
        ev_wqld = E("gpsimd", lambda e, c=c: e.dma_start(
            out=c_wq[:, c * D:(c + 1) * D], in_=d_wq[c * 128:(c + 1) * 128, :]),
            waits=(ev_kmm_last,) if c == 0 else (), inc=dgq, n=16)

    ev_c2w = E("sync", lambda e: e.dma_start(out=d_cc2i[:, :], in_=c_kvs[:]),
               waits=(ev_kv_all, wm(dt)), inc=dt, n=16)
    ev_ar2 = E("gpsimd", lambda e: e.collective_compute(
        "AllReduce", ALU.add, replica_groups=RG,
        ins=[d_cc2i.ap().opt()], outs=[d_cc2o.ap().opt()]), waits=(ev_c2w,), inc=scc)
    ev_c2r = E("sync", lambda e: e.dma_start(out=c_kvr[:], in_=d_cc2o[:, :]),
               waits=(ev_ar2, wm(dt)), inc=dt, n=16)

    prevq = None
    ev_qf = [None] * ND
    for hp in range(ND):
        w_q = [ev_wqld]
        if prevq is None:
            w_q += [prev["tr_ev"], prev["ks_ev"]]
        else:
            w_q += [prevq["qf_mm"], prevq["q_ev"]]
        ev_qmm = None
        for c in range(ND):
            for j in range(NTC):
                ev_qmm = E("tensor", lambda e, hp=hp, c=c, j=j: e.matmul(
                    c_ps[:, j * 512:(j + 1) * 512],
                    c_wq[:, c * D + hp * 128: c * D + hp * 128 + 128],
                    bass.AP(c_xn, (4 * j) * D + c * 128, [[ND * T, 128], [D, 4], [1, 128]]),
                    start=(c == 0), stop=(c == ND - 1)),
                    waits=tuple(w_q) if (c == 0 and j == 0) else (),
                    inc=(st if (c == ND - 1 and j == NTC - 1) else None))
        ev_qev_a = E("vector", lambda e: e.tensor_copy(
            c_sta[:, 0:T // 2], c_ps[:, 0:T // 2]), waits=(ev_qmm,), inc=sv)
        ev_qev = E("vector", lambda e: e.tensor_copy(
            c_sta[:, T // 2:T], c_ps[:, T // 2:T]), waits=(ev_qev_a,), inc=sv)
        ev_qfmm = None
        for j in range(NTC):
            w_qf = [ev_qev_a if j < NTC // 2 else ev_qev]
            if j == 0:
                if prevq is not None:
                    w_qf.append(prevq["qf_ev"])
                else:
                    w_qf += [prev["ks_ev"], prev["wd_ev"]]
            ev_qfmm = E("tensor", lambda e, j=j: e.matmul(
                c_ps[:, (4 + j) * 512:(5 + j) * 512], c_wfbd[:],
                c_sta[:, j * 512:(j + 1) * 512], start=True, stop=True),
                waits=tuple(w_qf) if j in (0, NTC // 2) else (),
                inc=(st if j == NTC - 1 else None))
        w_qfe = [ev_qfmm, ev_bfl] + ([ev_kv_all] if hp == 0 else [])
        ev_qf[hp] = E("scalar", lambda e, hp=hp: e.activation(
            c_qf[:, hp * T:(hp + 1) * T], c_ps[:, 4 * 512: 4 * 512 + T],
            AF.Relu, bias=c_bf[:], scale=1.0), waits=tuple(w_qfe), inc=ss)
        prevq = {"q_ev": ev_qev, "qf_ev": ev_qf[hp], "qf_mm": ev_qfmm}

    ev_sxv = E("vector", lambda e: e.tensor_tensor(
        sxb, gmxb, c_cols[:, COL["sigxb"]:COL["sigxb"] + 1], ALU.mult),
        waits=(ev_gmxb, ev_sig_bc), inc=sv)
    ev_rsxv = E("vector", lambda e: e.reciprocal(rsxb, sxb), waits=(ev_sxv,), inc=sv)
    ev_sxb = ev_rsxb = ev_rsxv

    wm_sv_A2 = wm(sv)
    _intx_state = {"prev": None, "q": [None] * NTC}

    def emit_intx_chunk(c, q, scratch=None, w_extra=()):
        QW = T // NTC
        lo = q * QW
        st_ = c_ixst[:, lo:lo + QW] if scratch is None else scratch[:, 0:QW]
        w_p1 = [ev_rsxb] + list(w_extra)
        if c == 0 and q == 0:
            w_p1 += [prevq["qf_mm"], ev_qf[ND - 1]]
        if scratch is None and _intx_state["q"][q] is not None:
            w_p1.append(_intx_state["q"][q])
        if scratch is not None and _intx_state["prev"] is not None:
            w_p1.append(_intx_state["prev"])
        ev_p1 = E("vector", lambda e, c=c, lo=lo, QW=QW, st_=st_: e.tensor_scalar(
            st_, c_xn[:, c * T + lo: c * T + lo + QW],
            rsxb, MAGIC, ALU.mult, ALU.add), waits=tuple(w_p1), inc=sv)
        ev_p2 = E("vector", lambda e, QW=QW, st_=st_: e.tensor_scalar(
            st_, st_, MAGIC, -QMAX - 1.0, ALU.subtract, ALU.max), waits=(ev_p1,), inc=sv)
        ev_p3 = E("vector", lambda e, c=c, lo=lo, QW=QW, st_=st_: e.tensor_scalar(
            c_intx[:, c * T + lo: c * T + lo + QW], st_,
            QMAX, None, ALU.min), waits=(ev_p2,), inc=sv)
        if scratch is None:
            _intx_state["q"][q] = ev_p3
        _intx_state["prev"] = ev_p3

    ev_wold = None
    for c in range(ND):
        ev_wold = E("gpsimd", lambda e, c=c: e.dma_start(
            out=c_wo[:, c * D:(c + 1) * D], in_=d_wo[c * 128:(c + 1) * 128, :]),
            waits=(prevq["qf_mm"],) if c == 0 else (), inc=dgo, n=16)

    prevC = None
    ev_ofm = {}
    ev_rn_w = {}

    def emit_ksr(hp, w_ksr):
        ksum_col = c_kvr[:, hp * 129 + 128: hp * 129 + 129]
        return E("scalar", lambda e, ksum_col=ksum_col: e.activation(
            c_ksr[:], c_mask[:], AF.Copy, bias=0.0, scale=ksum_col),
            waits=tuple(w_ksr), inc=ss)

    def emit_kvbd(hp, w_kvbd):
        return E("vector", lambda e, hp=hp: e.tensor_tensor(
            c_kvbd[:], c_kvr[:, hp * 129: hp * 129 + 128], c_mask[:], ALU.mult),
            waits=tuple(w_kvbd), inc=sv)

    cur_ksr = emit_ksr(0, [ev_c2r, ev_mask])
    cur_kvbd = emit_kvbd(0, [ev_c2r, ev_mask])
    ev_rn_sv = {}
    ev_multd = {}

    def rn_of(hp):
        return c_rn if hp % 2 == 0 else c_rn2

    def emit_mult(hp, j):
        rb = rn_of(hp)
        ev_multd[(hp, j)] = E("vector", lambda e, hp=hp, j=j, rb=rb: e.tensor_tensor(
            c_ofm[:, hp * T + j * 512: hp * T + (j + 1) * 512],
            c_ofm[:, hp * T + j * 512: hp * T + (j + 1) * 512],
            rb[:, j * 512:(j + 1) * 512], ALU.mult),
            waits=(ev_rn_sv[(hp, j)], ev_ofm[(hp, j)]), inc=sv)

    for hp in range(ND):
        ev_ksr, ev_kvbd = cur_ksr, cur_kvbd
        rb = rn_of(hp)
        ev_rn = {}
        ev_rnm_d = {}
        for j in range(NTC):
            bslot = j % 2
            w_n = [ev_ksr, ev_qf[hp]]
            if prevC is None and j < 2:
                w_n += [prevq["q_ev"]]
            if prevC is not None and j < 2:
                w_n.append(prevC["rnm"][min(2 + (j % 2), NTC - 1)])
            if j >= 2:
                w_n.append(ev_rnm_d[j - 2])
            ev_nmm = E("tensor", lambda e, hp=hp, j=j, bslot=bslot: e.matmul(
                c_ps[:, bslot * 512:(bslot + 1) * 512], c_ksr[:],
                c_qf[:, hp * T + j * 512: hp * T + (j + 1) * 512], start=True, stop=True),
                waits=tuple(w_n), inc=st)
            w_rnm = [ev_nmm]
            if (hp - 2, j) in ev_multd:
                w_rnm.append(ev_multd[(hp - 2, j)])
            ev_rnm = E("vector", lambda e, j=j, bslot=bslot, rb=rb: e.tensor_scalar(
                rb[:, j * 512:(j + 1) * 512], c_ps[:, bslot * 512:(bslot + 1) * 512],
                1e-6, None, ALU.max), waits=tuple(w_rnm), inc=sv)
            ev_rnm_d[j] = ev_rnm
            if hp >= 1:
                emit_mult(hp - 1, j)

            def emit_lnexp(j, ev_rnm):
                ev_rln = E("scalar", lambda e, j=j, rb=rb: e.activation(
                    rb[:, j * 512:(j + 1) * 512], rb[:, j * 512:(j + 1) * 512],
                    AF.Ln), waits=(ev_rnm,), inc=ss)
                ev_rn_sv[(hp, j)] = E("scalar", lambda e, j=j, rb=rb: e.activation(
                    rb[:, j * 512:(j + 1) * 512], rb[:, j * 512:(j + 1) * 512],
                    AF.Exp, scale=-1.0), waits=(ev_rln,), inc=ss)

            if j < NTC - 1:
                emit_lnexp(j, ev_rnm)
        if hp + 1 < ND:
            cur_ksr = emit_ksr(hp + 1, [ev_nmm])
        emit_lnexp(NTC - 1, ev_rnm_d[NTC - 1])
        ev_o_last = None
        for j in range(NTC):
            bslot = 2 + j % 2
            w_o = [ev_kvbd, ev_qf[hp]]
            if prevC is not None and j < 2:
                w_o.append(prevC["o_last"])
            if j >= 2:
                w_o.append(ev_ofm[(hp, j - 2)])
            if prevC is None and hp == 0 and j == 0:
                w_o += [prevq["qf_mm"], ev_c2w]
            ev_omm = E("tensor", lambda e, hp=hp, j=j, bslot=bslot: e.matmul(
                c_ps[:, bslot * 512:(bslot + 1) * 512], c_kvbd[:],
                c_qf[:, hp * T + j * 512: hp * T + (j + 1) * 512], start=True, stop=True),
                waits=tuple(w_o), inc=st)
            ev_ofm[(hp, j)] = E("vector", lambda e, hp=hp, j=j, bslot=bslot: e.tensor_copy(
                c_ofm[:, hp * T + j * 512: hp * T + (j + 1) * 512],
                c_ps[:, bslot * 512:(bslot + 1) * 512]), waits=(ev_omm,), inc=sv)
            ev_o_last = ev_ofm[(hp, j)]
        prevC = {"out_mm": ev_omm, "o_last": ev_o_last, "rnm": ev_rnm_d}
        if hp + 1 < ND:
            cur_kvbd = emit_kvbd(hp + 1, [ev_omm])
    for j in range(NTC):
        emit_mult(ND - 1, j)
    ev_mult_last = ev_multd[(ND - 1, NTC - 1)]
    ev_intx_hp = _intx_state["prev"]
    cs_def = list(range(ND))
    intx_jobs = [(c, q) for c in cs_def for q in range(NTC)]
    intx_done_n = [0]
    wm_sv_ofm = wm(sv)

    wm_st_hpC = wm(st)
    ev_sld = {}
    prev_g, prev_u = ev_wold, None
    for k in range(2):
        ev_sld[("g", k)] = prev_g = E(
            "gpsimd", lambda e, k=k: e.dma_start(
                out=c_wgs[k][:], in_=d_wg[k * 128:(k + 1) * 128, :]),
            waits=(wm_st_hpC, prev_g), inc=dgo, n=16)
        ev_sld[("u", k)] = prev_u = E(
            "gpsimd", lambda e, k=k: e.dma_start(
                out=c_wus[k][:], in_=d_wu[k * 128:(k + 1) * 128, :]),
            waits=(wm_st_hpC,) + ((prev_u,) if prev_u is not None else ()),
            inc=dwr, n=16)

    ev_mh = None
    evB = None
    ev_wld_free = {}
    ev_tqw_free = None
    ev_wqn_read = None
    ev_slab_gread = None
    ev_slab_uread = None
    ev_quant = {}

    def emit_wload(jc):
        if jc >= NI:
            return
        if jc < 2:
            for wname in ("g", "u"):
                ev_quant[(wname, "ld", jc)] = ev_sld[(wname, jc)]
            return
        par = jc % 2
        for wname, dsrc, bufs, dsems in (("g", d_wg, c_wgn, dswg), ("u", d_wu, c_wun, dswu)):
            wnat = bufs[par]
            w_ld = [ev_wld_free[(wname, jc - 2)]]
            if jc < 4:
                w_ld.append(wm_st_C)
            ev_quant[(wname, "ld", jc)] = E(
                "sync", lambda e, jc=jc, dsrc=dsrc, wnat=wnat: e.dma_start(
                    out=wnat[:], in_=dsrc[jc * 128:(jc + 1) * 128, :]),
                waits=tuple(w_ld), inc=dsems[par], n=16)

    def emit_quant(jc):
        if ("g", jc) in ev_quant:
            return
        for wname in ("g", "u"):
            par = jc % 2
            if jc < 2:
                wnat = (c_wgs if wname == "g" else c_wus)[jc]
                tqw = c_tqw01[:, 0:D]
            else:
                wnat = (c_wgn if wname == "g" else c_wun)[par]
                tqw = c_tqw[:, 0:D]
            if jc < 2:
                wqn = (c_wqng01 if wname == "g" else c_wqnu01)[jc]
            else:
                wqn = c_wqng if wname == "g" else c_wqnu
            wmx = col(f"w{wname}m{par}")
            r0 = col(f"w{wname}r0{par}")
            rsw = col(f"w{wname}rs{par}")
            sw = col(f"w{wname}sw{par}")
            ev_wm = E("vector", lambda e, wnat=wnat, wmx=wmx: e.tensor_reduce(
                wmx, wnat[:], axis=AX, op=ALU.max, apply_absolute_value=True),
                waits=(ev_quant[(wname, "ld", jc)],), inc=sv)
            ev_r0 = E("vector", lambda e, wmx=wmx, r0=r0: e.reciprocal(r0, wmx),
                      waits=(ev_wm,), inc=sv)
            ev_rsw = E("vector", lambda e, r0=r0, rsw=rsw: e.tensor_scalar(
                rsw, r0, QMAX, None, ALU.mult), waits=(ev_r0,), inc=sv)
            ev_sw = E("vector", lambda e, wmx=wmx, sw=sw: e.tensor_scalar(
                sw, wmx, 1.0 / QMAX, None, ALU.mult), waits=(ev_rsw,), inc=sv)
            w_p1 = [ev_sw]
            if jc >= 2:
                if ev_tqw_free2[0] is not None:
                    w_p1.append(ev_tqw_free2[0])
                if jc == 2 and wname == "g":
                    w_p1 += [ev_intx_done, wm_dsy0]
            ev_p1 = E("vector", lambda e, wnat=wnat, rsw=rsw, tqw=tqw: e.tensor_scalar(
                tqw, wnat[:], rsw, MAGIC, ALU.mult, ALU.add),
                waits=tuple(w_p1), inc=sv)
            ev_wld_free[(wname, jc)] = ev_p1
            w_p2 = [ev_p1] + ([ev_wqn_read2[0]] if ev_wqn_read2[0] is not None else [])
            if jc == 2:
                w_p2.append(wm_dsy1)
            ev_p2 = E("vector", lambda e, wqn=wqn, sw=sw, tqw=tqw: e.tensor_scalar(
                wqn[:], tqw, MAGIC, sw, ALU.subtract, ALU.mult),
                waits=tuple(w_p2), inc=sv)
            if jc >= 2:
                ev_tqw_free2[0] = ev_p2
            ev_quant[(wname, jc)] = ev_p2

    ev_tqw_free2 = [None]
    ev_wqn_read2 = [None]

    ev_ymm = [None, None]
    ev_yst = [None, None]
    ev_yst4 = [None] * 4
    ev_ye_b03 = None
    ev_ytw = [None] * NT
    for tt in range(NT):
        par = tt % 2
        slot = tt % 4
        b0 = slot * 2
        ev_amm = None
        for hp in range(ND):
            for h in range(NVH):
                lo, hi = h * 512, min(D, (h + 1) * 512)
                w_a = []
                if hp == 0 and h == 0:
                    w_a = [ev_wold, ev_mult_last]
                    if ev_yst4[slot] is not None:
                        w_a.append(ev_yst4[slot])
                    elif slot >= 2:
                        w_a.append(ev_qf[ND - 1])
                ev_amm = E("tensor", lambda e, tt=tt, hp=hp, lo=lo, hi=hi, b0=b0: e.matmul(
                    c_ps[:, b0 * 512 + lo: b0 * 512 + hi],
                    c_ofm[:, hp * T + tt * 128: hp * T + tt * 128 + 128],
                    c_wo[:, hp * D + lo: hp * D + hi],
                    start=(hp == 0), stop=(hp == ND - 1)),
                    waits=tuple(w_a) if (hp == 0 and h == 0) else (),
                    inc=(st if (hp == ND - 1 and h == NVH - 1) else None))
        ev_ymm[par] = ev_amm
        yst = c_yst0 if par == 0 else c_yst1
        w_ev = [ev_amm]
        if tt < 2 and ev_intx_hp is not None:
            w_ev.append(ev_intx_hp)
        if tt >= 2 and ev_ytw[tt - 2] is not None:
            w_ev.append(ev_ytw[tt - 2])
        ev_ye = E("scalar", lambda e, yst=yst, b0=b0: e.copy(
            yst[:], c_ps[:, b0 * 512: b0 * 512 + D]), waits=tuple(w_ev), inc=ss)
        ev_yst[par] = ev_ye
        ev_yst4[slot] = ev_ye
        if tt == NT - 3:
            ev_ye_b03 = ev_ye
        ev_ytw[tt] = E("sync", lambda e, tt=tt, yst=yst: e.dma_start(
            out=d_ya[tt * 128:(tt + 1) * 128, :], in_=yst[:]), waits=(ev_ye,), inc=dsy[par], n=16)
        if tt == 2:
            emit_wload(0)
            emit_wload(1)
        if tt in (3, 5):
            emit_quant(0 if tt == 3 else 1)
        if tt >= 1:
            n_do = (len(intx_jobs) * min(tt, NT - 4)) // (NT - 4) - intx_done_n[0]
            for _ in range(n_do):
                cdef, q = intx_jobs[intx_done_n[0]]
                emit_intx_chunk(cdef, q, scratch=c_ixst2,
                                w_extra=(wm_st_hpC,) if intx_done_n[0] == 0 else ())
                intx_done_n[0] += 1
    ev_intx_done = _intx_state["prev"]
    ev_ye_last = ev_yst[(NT - 1) % 2]
    wm_st_C = wm(st)
    wm_ss_C = wm(ss)
    wm_sv_C = wm(sv)
    wm_dsy0 = wm(dsya)
    wm_dsy1 = wm(dsyb)

    wm_sgp_C = wm(sgp)

    mha = col("mhacc")
    def emit_slab(ic, ev_silu_prev):
        ev_trB = None
        for wname, base in (("g", 0), ("u", 1024)):
            if ic < 2:
                wqn = (c_wqng01 if wname == "g" else c_wqnu01)[ic]
            else:
                wqn = c_wqng if wname == "g" else c_wqnu
            for c in range(ND):
                w_t = [ev_quant[(wname, ic)], ev_ident]
                if wname == "g" and c == 0:
                    if ev_silu_prev is not None:
                        w_t.append(ev_silu_prev)
                    else:
                        w_t += [ev_mult_last, ev_ye_b03]
                ev_trB = E("tensor", lambda e, wqn=wqn, c=c, base=base: e.matmul(
                    c_ps[:, base + c * 128: base + c * 128 + 128],
                    wqn[:, c * 128:(c + 1) * 128], c_ident[:], start=True, stop=True),
                    waits=tuple(w_t) if c == 0 else (),
                    inc=(st if (wname == "u" and c == ND - 1) else None))
        ev_wqn_read2[0] = ev_trB
        w_se = [ev_trB] + ([ev_slab_gread] if ev_slab_gread is not None else [])
        ev_sgev = E("scalar", lambda e: e.copy(
            c_slabg[:, 0:ND * 128], c_ps[:, 0:ND * 128]),
            waits=tuple(w_se) + ((ev_ye_last,) if ic == 0 else ()), inc=ss)
        w_su = [ev_trB] + ([ev_slab_uread] if ev_slab_uread is not None else [])
        ev_suev = E("vector", lambda e: e.tensor_copy(
            c_slabu[:, 0:ND * 128], c_ps[:, 1024:1024 + ND * 128]),
            waits=tuple(w_su) + ((ev_ye_last,) if ic == 0 else ()), inc=sv)
        return ev_sgev, ev_suev

    emit_wload(0)
    emit_wload(1)
    emit_quant(0)
    cur_sgev, cur_suev = emit_slab(0, None)
    for ic in range(NI):
        emit_wload(ic + 2)
        ev_sgev, ev_suev = cur_sgev, cur_suev
        ev_gmm = None
        for c in range(ND):
            for j in range(NTC):
                w_g = []
                if c == 0 and j == 0:
                    w_g = [ev_sgev, ev_suev, ev_intx_done]
                ev_gmm = E("tensor", lambda e, c=c, j=j: e.matmul(
                    c_ps[:, j * 512:(j + 1) * 512], c_slabg[:, c * 128:(c + 1) * 128],
                    bass.AP(c_intx, (4 * j) * D + c * 128, [[ND * T, 128], [D, 4], [1, 128]]),
                    start=(c == 0), stop=(c == ND - 1)),
                    waits=tuple(w_g) if (c == 0 and j == 0) else (),
                    inc=(st if (c == ND - 1 and j == NTC - 1) else None))
        ev_umm = None
        for c in range(ND):
            for j in range(NTC):
                w_u = []
                if c == 0 and j == 0:
                    w_u = [ev_suev]
                    if evB is not None:
                        w_u.append(evB["stt_ev"])
                    else:
                        w_u += [wm_sv_C, wm_ss_C]
                ev_umm = E("tensor", lambda e, c=c, j=j: e.matmul(
                    c_ps[:, (4 + j) * 512:(5 + j) * 512], c_slabu[:, c * 128:(c + 1) * 128],
                    bass.AP(c_intx, (4 * j) * D + c * 128, [[ND * T, 128], [D, 4], [1, 128]]),
                    start=(c == 0), stop=(c == ND - 1)),
                    waits=tuple(w_u) if (c == 0 and j == 0) else (),
                    inc=(st if (c == ND - 1 and j == NTC - 1) else None))
        ev_slab_gread, ev_slab_uread = ev_gmm, ev_umm
        if ic + 1 < NI:
            emit_quant(ic + 1)
        w_si = [ev_gmm, ev_sxb]
        if evB is not None:
            w_si.append(evB["stt_ev"])
        else:
            w_si += [wm_ss_C, wm_sv_C, wm(dt)]
        ev_silu = E("scalar", lambda e: e.activation(
            c_sg[:], c_ps[:, 0:T], AF.Silu, scale=sxb), waits=tuple(w_si), inc=ss)
        if ic + 1 < NI:
            cur_sgev, cur_suev = emit_slab(ic + 1, ev_silu)
        w_st_ = [ev_silu, ev_umm]
        if ic == 0:
            w_st_ += [wm_dsy0, wm_dsy1]
            w_st_ += [wm_st_C, wm_sv_C]
        hm = col("hm")
        if ic < NI - 1:
            ev_stt = E("vector", lambda e, ic=ic: e.scalar_tensor_tensor(
                c_hpre[:, ic * T:(ic + 1) * T], c_ps[:, 4 * 512: 4 * 512 + T], sxb, c_sg[:],
                ALU.mult, ALU.mult), waits=tuple(w_st_), inc=sv)
            ev_hmr = E("vector", lambda e, ic=ic, hm=hm: e.tensor_reduce(
                hm, c_hpre[:, ic * T:(ic + 1) * T], axis=AX, op=ALU.max,
                apply_absolute_value=True), waits=(ev_stt,), inc=sv)
            if ic == 0:
                ev_mh = E("vector", lambda e, hm=hm: e.tensor_copy(mha, hm),
                          waits=(ev_hmr,), inc=sv)
            else:
                ev_mh = E("vector", lambda e, hm=hm: e.tensor_tensor(
                    mha, mha, hm, ALU.max), waits=(ev_hmr, ev_mh), inc=sv)
        else:
            hm2 = col("hm2")
            ev_stt_a = E("vector", lambda e, ic=ic: e.scalar_tensor_tensor(
                c_hpre[:, ic * T: ic * T + T // 2],
                c_ps[:, 4 * 512: 4 * 512 + T // 2], sxb, c_sg[:, 0:T // 2],
                ALU.mult, ALU.mult), waits=tuple(w_st_), inc=sv)
            ev_hm_a = E("vector", lambda e, ic=ic, hm=hm: e.tensor_reduce(
                hm, c_hpre[:, ic * T: ic * T + T // 2], axis=AX, op=ALU.max,
                apply_absolute_value=True), waits=(ev_stt_a,), inc=sv)
            ev_stt = E("vector", lambda e, ic=ic: e.scalar_tensor_tensor(
                c_hpre[:, ic * T + T // 2:(ic + 1) * T],
                c_ps[:, 4 * 512 + T // 2: 4 * 512 + T], sxb, c_sg[:, T // 2:T],
                ALU.mult, ALU.mult), waits=(), inc=sv)
            ev_hm_b = E("vector", lambda e, ic=ic, hm2=hm2: e.tensor_reduce(
                hm2, c_hpre[:, ic * T + T // 2:(ic + 1) * T], axis=AX, op=ALU.max,
                apply_absolute_value=True), waits=(ev_stt,), inc=sv)
            ev_mh = E("vector", lambda e, hm=hm: e.tensor_tensor(
                mha, mha, hm, ALU.max), waits=(ev_hm_a, ev_mh), inc=sv)
            ev_mh = E("vector", lambda e, hm2=hm2: e.tensor_tensor(
                mha, mha, hm2, ALU.max), waits=(ev_hm_b, ev_mh), inc=sv)
        evB = {"silu_ev": ev_silu, "stt_ev": ev_stt}
    wm_st_B = wm(st)
    wm_ss_B = wm(ss)

    ev_r3mm = E("tensor", lambda e: e.matmul(
        c_ps[0:1, 4 * 512: 4 * 512 + 128], mha, c_identf[:], start=True, stop=True),
        waits=(ev_mh, evB["stt_ev"]), inc=st)
    gmh1 = c_s1[:, 1:2]
    ev_ar3row = E("vector", lambda e: e.tensor_reduce(
        gmh1, c_ps[0:1, 4 * 512: 4 * 512 + 128], axis=AX, op=ALU.max),
        waits=(ev_r3mm,), inc=sv)
    ev_pre3 = E("vector", lambda e: e.tensor_scalar(
        gmh1, gmh1, c_cols[0:1, COL["sigdb"]:COL["sigdb"] + 1], None, ALU.mult),
        waits=(ev_ar3row, ev_sig_bc), inc=sv)
    ev_c3w = E("gpsimd", lambda e: e.dma_start(out=d_cc3i[None, :], in_=gmh1),
               waits=(ev_pre3, wm(dbx)), inc=dbx, n=16)
    ev_ar3 = E("gpsimd", lambda e: e.collective_compute(
        "AllReduce", ALU.max, replica_groups=RG,
        ins=[d_cc3i.ap().opt()], outs=[d_cc3o.ap().opt()]), waits=(ev_c3w,), inc=scc)
    rshb = col("rshb")
    gmhb = col("gmhb")
    shb = gmhb
    ev_gmhb = E("gpsimd", lambda e: e.dma_start(
        out=gmhb, in_=bass.AP(d_cc3o, 0, [[0, 128], [1, 1]])),
        waits=(ev_ar3,), inc=dbh, n=16)
    ev_rshv = E("vector", lambda e: e.reciprocal(rshb, gmhb), waits=(ev_gmhb,), inc=sv)
    ev_shb = ev_rshb = ev_rshv

    c_bo2 = sb("bo2", [128, D], F32, 94 * KB + 20 * D)
    ev_bo2 = E("sync", lambda e: e.dma_start(out=c_bo2[:], in_=d_bo[:, :]),
               waits=(wm_st_B, wm(dsc)), inc=dsc, n=16)

    segb = [0, NT // 16, NT // 8, NT // 4, NT // 2, (3 * NT) // 4, NT]
    segb = sorted(set(segb))
    NSEG = len(segb) - 1
    seg_of_tile = [0] * NT
    for si in range(NSEG):
        for tt in range(segb[si], segb[si + 1]):
            seg_of_tile[tt] = si
    ev_ih = [[None] * NI for _ in range(NSEG)]
    _cv = {"tq_free": [None, None], "first": True}

    def emit_conv(si, ic):
        lo = segb[si] * 128
        w = (segb[si + 1] - segb[si]) * 128
        par = (si * NI + ic) % 2
        tqs = c_tq[:, par * 512: par * 512 + w]
        w_p1 = [ev_rshb]
        if _cv["tq_free"][par] is not None:
            w_p1.append(_cv["tq_free"][par])
        else:
            w_p1 += [wm_dsy0, wm_dsy1, evB["stt_ev"], wm_ss_B]
        ev_p1 = E("scalar", lambda e, ic=ic, lo=lo, w=w, tqs=tqs: e.activation(
            tqs, c_hpre[:, ic * T + lo: ic * T + lo + w],
            AF.Copy, bias=MAGIC, scale=rshb), waits=tuple(w_p1), inc=ss)
        w_p2 = [ev_p1]
        if _cv["first"]:
            w_p2.append(ev_mh)
            _cv["first"] = False
        ev_p2 = E("vector", lambda e, tqs=tqs: e.tensor_scalar(
            tqs, tqs, MAGIC, -QMAX - 1.0, ALU.subtract, ALU.max),
            waits=tuple(w_p2), inc=sv)
        ev_p3 = E("vector", lambda e, ic=ic, lo=lo, w=w, tqs=tqs: e.tensor_scalar(
            c_hpre[:, ic * T + lo: ic * T + lo + w], tqs,
            QMAX, None, ALU.min), waits=(ev_p2,), inc=sv)
        _cv["tq_free"][par] = ev_p3
        ev_ih[si][ic] = ev_p3

    conv_jobs = [(si, ic) for si in range(NSEG) for ic in range(NI)]
    conv_ptr = 0

    def conv_upto(n):
        nonlocal_dummy = None
        return n

    ev_dyst = [None, None]
    ev_yrd = [None, None]
    while conv_ptr < 2 * NI:
        emit_conv(*conv_jobs[conv_ptr])
        conv_ptr += 1
    for tt in range(NT):
        par = tt % 2
        half = seg_of_tile[tt]
        b0 = 0 if par == 0 else 2
        if half + 1 < NSEG:
            segw = segb[half + 1] - segb[half]
            frac = (tt - segb[half] + 1) / segw
            target = min((half + 2) * NI, (half + 1) * NI + int(NI * frac + 0.999))
        else:
            target = NSEG * NI
        target = max(target, (half + 1) * NI)
        while conv_ptr < target:
            emit_conv(*conv_jobs[conv_ptr])
            conv_ptr += 1
        w_rb = [wm_dsy0, wm_dsy1, wm_st_B]
        if ev_yrd[par] is not None:
            w_rb.append(ev_yrd[par])
        ev_acc = E("sync", lambda e, tt=tt, par=par: e.dma_start(
            out=c_accrb[:, par * D:(par + 1) * D], in_=d_x[tt * 128:(tt + 1) * 128, :]),
            waits=tuple(w_rb), inc=dsr[par], n=16)
        ev_yar = E("sync", lambda e, tt=tt, par=par: e.dma_start(
            out=c_yab[:, par * D:(par + 1) * D], in_=d_ya[tt * 128:(tt + 1) * 128, :]),
            waits=tuple(w_rb), inc=dsx[par], n=16)
        ev_ya1 = E("gpsimd", lambda e, par=par: e.tensor_tensor(
            c_accrb[:, par * D:(par + 1) * D], c_accrb[:, par * D:(par + 1) * D],
            c_bo2[:], ALU.add), waits=(ev_acc, ev_bo2), inc=sgp)
        ev_yadd = E("gpsimd", lambda e, par=par: e.tensor_tensor(
            c_accrb[:, par * D:(par + 1) * D], c_accrb[:, par * D:(par + 1) * D],
            c_yab[:, par * D:(par + 1) * D], ALU.add),
            waits=(ev_ya1, ev_yar), inc=sgp)
        ev_dm = None
        for ic in range(NI):
            for h in range(NVH):
                lo, hi = h * 512, min(D, (h + 1) * 512)
                w_d = []
                if h == 0:
                    w_d = [ev_ih[half][ic]]
                if ic == 0 and h == 0:
                    w_d.append(ev_wd_done)
                    w_d.append(ev_yrd[par] if ev_yrd[par] is not None else wm_ss_B)
                ev_dm = E("tensor", lambda e, tt=tt, ic=ic, lo=lo, hi=hi, b0=b0: e.matmul(
                    c_ps[:, b0 * 512 + lo: b0 * 512 + hi],
                    c_hpre[:, ic * T + tt * 128: ic * T + tt * 128 + 128],
                    c_wdqT[:, ic * D + lo: ic * D + hi],
                    start=(ic == 0), stop=(ic == NI - 1)),
                    waits=tuple(w_d),
                    inc=(st if (ic == NI - 1 and h == NVH - 1) else None))
        w_ev = [ev_dm, ev_yadd, ev_shb]
        if ev_yrd[par] is not None:
            w_ev.append(ev_yrd[par])
        if ev_dyst[par] is not None:
            w_ev.append(ev_dyst[par])
        ev_ye = E("vector", lambda e, par=par, b0=b0: e.scalar_tensor_tensor(
            c_ytile[:, par * D:(par + 1) * D], c_ps[:, b0 * 512: b0 * 512 + D], shb,
            c_accrb[:, par * D:(par + 1) * D], ALU.mult, ALU.add), waits=tuple(w_ev), inc=sv)
        ev_yrd[par] = ev_ye
        ev_dyst[par] = E("sync", lambda e, tt=tt, par=par: e.dma_start(
            out=d_y[tt * 128:(tt + 1) * 128, :], in_=c_ytile[:, par * D:(par + 1) * D]),
            waits=(ev_ye,), inc=dsy2[par], n=16)

    for semx in (dsc, dsx0, dsx1, dsx2, dsx3, dsw0, dsw1, dsp, dsya, dsyb,
                 dwr, dswg0, dswg1, dswu0, dswu1, dsra, dsrb, dsy2a, dsy2b,
                 dt, dbx, dbh, dbs):
        E("sync", lambda e: None, waits=(wm(semx),))
    E("gpsimd", lambda e: None, waits=(wm(dgv), wm(dgk), wm(dgq), wm(dgo), wm(scc)))

    with stack:
        with contextlib.ExitStack() as semctx:
            for s in sems:
                s.h = semctx.enter_context(nc.semaphore(s.name))
            with nc.Block() as block:
                @block.tensor
                def _(eng):
                    for f in queues["tensor"]:
                        f(eng)

                @block.vector
                def _(eng):
                    for f in queues["vector"]:
                        f(eng)

                @block.scalar
                def _(eng):
                    for f in queues["scalar"]:
                        f(eng)

                @block.sync
                def _(eng):
                    for f in queues["sync"]:
                        f(eng)

                @block.gpsimd
                def _(eng):
                    for f in queues["gpsimd"]:
                        f(eng)
    return nc


def _shard_inputs(cfg, L0_attn_x, L0_attn_Wq, L0_attn_Wk, L0_attn_Wv, L0_attn_Wo,
                  L0_attn_bo, L0_attn_Wf, L0_attn_bf, L0_ffn_clip_x, L0_ffn_clip_down,
                  L0_ffn_w_gate, L0_ffn_w_up, L0_ffn_w_down):
    T, D = cfg.T, cfg.D
    f32 = np.float32
    x = np.asarray(L0_attn_x, f32).reshape(-1, D)
    wf = np.asarray(L0_attn_Wf, f32)
    fd = wf.shape[0]
    wfbd = np.zeros((128, 128), f32)
    wfbd[0:fd, 0:fd] = wf
    wfbd[fd:2 * fd, fd:2 * fd] = wf
    bf = np.asarray(L0_attn_bf, f32)
    bfr = np.concatenate([bf, bf]).reshape(128, 1).astype(f32)
    bor = np.ascontiguousarray(np.broadcast_to(np.asarray(L0_attn_bo, f32), (128, D)))
    mask = np.zeros((128, 128), f32)
    mask[0:fd, 0:fd] = 1.0
    mask[fd:2 * fd, fd:2 * fd] = 1.0
    common = {
        "wq": np.ascontiguousarray(np.asarray(L0_attn_Wq, f32)),
        "wk": np.ascontiguousarray(np.asarray(L0_attn_Wk, f32)),
        "wv": np.ascontiguousarray(np.asarray(L0_attn_Wv, f32)),
        "wo": np.ascontiguousarray(np.asarray(L0_attn_Wo, f32)),
        "wfbd": wfbd, "bfr": bfr, "bor": bor, "maskbd": mask,
        "ident": np.eye(128, dtype=ml_dtypes.bfloat16),
        "identf": np.eye(128, dtype=np.float32),
        "onescol": np.ones((128, 1), dtype=ml_dtypes.bfloat16),
        "clip2": np.array([np.asarray(L0_ffn_clip_x, f32).ravel()[0],
                           np.asarray(L0_ffn_clip_down, f32).ravel()[0]], f32),
        "wg": np.ascontiguousarray(np.asarray(L0_ffn_w_gate, f32)),
        "wu": np.ascontiguousarray(np.asarray(L0_ffn_w_up, f32)),
        "wd": np.ascontiguousarray(np.asarray(L0_ffn_w_down, f32)),
    }
    in_maps = []
    for c in range(cfg.n_cores):
        m = dict(common)
        m["x"] = np.ascontiguousarray(x[c * T:(c + 1) * T, :])
        in_maps.append(m)
    return in_maps


def kernel(L0_attn_x, L0_attn_Wq, L0_attn_Wk, L0_attn_Wv, L0_attn_Wo, L0_attn_bo,
           L0_attn_Wf, L0_attn_bf, L0_ffn_clip_x, L0_ffn_clip_down, L0_ffn_w_gate,
           L0_ffn_w_up, L0_ffn_w_down):
    cfg = REAL
    in_maps = _shard_inputs(cfg, L0_attn_x, L0_attn_Wq, L0_attn_Wk, L0_attn_Wv,
                            L0_attn_Wo, L0_attn_bo, L0_attn_Wf, L0_attn_bf,
                            L0_ffn_clip_x, L0_ffn_clip_down, L0_ffn_w_gate,
                            L0_ffn_w_up, L0_ffn_w_down)
    nc = build_program(cfg)
    res = run_bass_kernel_spmd(nc, in_maps, core_ids=list(range(cfg.n_cores)))
    y = np.concatenate([np.asarray(r["y"], np.float32) for r in res.results], axis=0)
    return y.reshape(1, cfg.n_cores * cfg.T, cfg.D)


# version 54
# speedup vs baseline: 1.0176x; 1.0176x over previous
import contextlib
import numpy as np
import ml_dtypes
import concourse.bass as bass
import concourse.mybir as mybir
from concourse.bass_utils import run_bass_kernel_spmd

F32 = mybir.dt.float32
BF16 = mybir.dt.bfloat16
AF = mybir.ActivationFunctionType
ALU = mybir.AluOpType
AX = mybir.AxisListType.X
MAGIC = float(np.float32(3 * 2.0 ** 22))
QMAX = 127.0
KB = 1024


class Cfg:
    def __init__(self, n_cores=8, T=2048, D=1024, I=2816):
        self.n_cores, self.T, self.D, self.I = n_cores, T, D, I
        assert T % 512 == 0 and D % 128 == 0 and I % 128 == 0
        self.NT, self.ND, self.NI, self.NTC = T // 128, D // 128, I // 128, T // 512


REAL = Cfg()


class PSem:
    def __init__(self, name):
        self.name, self.c, self.h = name, 0, None


def build_program(cfg: Cfg):
    nc = bass.Bass("TRN2", target_bir_lowering=False)
    T, D, I = cfg.T, cfg.D, cfg.I
    NT, ND, NI, NTC = cfg.NT, cfg.ND, cfg.NI, cfg.NTC
    NVH = (D + 511) // 512
    TH = T // 2

    d_x = nc.dram_tensor("x", [T, D], F32, kind="ExternalInput")
    d_wq = nc.dram_tensor("wq", [D, D], F32, kind="ExternalInput")
    d_wk = nc.dram_tensor("wk", [D, D], F32, kind="ExternalInput")
    d_wv = nc.dram_tensor("wv", [D, D], F32, kind="ExternalInput")
    d_wo = nc.dram_tensor("wo", [D, D], F32, kind="ExternalInput")
    d_wfbd = nc.dram_tensor("wfbd", [128, 128], F32, kind="ExternalInput")
    d_bf = nc.dram_tensor("bfr", [128, 1], F32, kind="ExternalInput")
    d_bo = nc.dram_tensor("bor", [128, D], F32, kind="ExternalInput")
    d_mask = nc.dram_tensor("maskbd", [128, 128], F32, kind="ExternalInput")
    d_ident = nc.dram_tensor("ident", [128, 128], BF16, kind="ExternalInput")
    d_ones = nc.dram_tensor("onescol", [128, 1], BF16, kind="ExternalInput")
    d_clip = nc.dram_tensor("clip2", [2], F32, kind="ExternalInput")
    d_wg = nc.dram_tensor("wg", [I, D], F32, kind="ExternalInput")
    d_wu = nc.dram_tensor("wu", [I, D], F32, kind="ExternalInput")
    d_wd = nc.dram_tensor("wd", [D, I], F32, kind="ExternalInput")
    d_y = nc.dram_tensor("y", [T, D], F32, kind="ExternalOutput")
    d_ya = nc.dram_tensor("ya_i", [T, D], BF16)
    d_wdsp = nc.dram_tensor("wdspill", [128, NI * D], BF16)
    d_cc1i = nc.dram_tensor("cc1i", [1], F32)
    d_cc1o = nc.dram_tensor("cc1o", [1], F32)
    d_cc2i = nc.dram_tensor("cc2i", [128, ND * 129], F32)
    d_cc2o = nc.dram_tensor("cc2o", [128, ND * 129], F32)
    d_cc3i = nc.dram_tensor("cc3i", [1], F32)
    d_cc3o = nc.dram_tensor("cc3o", [1], F32)
    d_identf = nc.dram_tensor("identf", [128, 128], F32, kind="ExternalInput")
    d_bc1 = nc.dram_tensor("bc1", [2], F32)
    d_bc3 = nc.dram_tensor("bc3", [2], F32)

    sems = [PSem(n) for n in (
        "st", "sv", "ss", "sgp", "dt", "scc", "dsc",
        "dsx0", "dsx1", "dsx2", "dsx3", "dsw0", "dsw1", "dsp",
        "dgv", "dgk", "dgq", "dgo", "dsya", "dsyb", "dwr",
        "dswg0", "dswg1", "dswu0", "dswu1", "dsra", "dsrb", "dsy2a", "dsy2b",
        "dbx", "dbh", "dbs")]
    (st, sv, ss, sgp, dt, scc, dsc, dsx0, dsx1, dsx2, dsx3, dsw0, dsw1, dsp,
     dgv, dgk, dgq, dgo, dsya, dsyb, dwr, dswg0, dswg1, dswu0, dswu1,
     dsra, dsrb, dsy2a, dsy2b, dbx, dbh, dbs) = sems
    dswg = [dswg0, dswg1]
    dswu = [dswu0, dswu1]
    dsx = [dsx0, dsx1, dsx2, dsx3]
    dsw = [dsw0, dsw1]
    dsy = [dsya, dsyb]
    dsr = [dsra, dsrb]
    dsy2 = [dsy2a, dsy2b]
    queues = {e: [] for e in ("tensor", "vector", "scalar", "sync", "gpsimd")}

    def E(eng, fn, waits=(), inc=None, n=1):
        waits = tuple((s, v) for (s, v) in waits if s is not None and v > 0)
        if inc is not None:
            inc.c += n

        def do(e, waits=waits, fn=fn, inc=inc, n=n):
            for s, v in waits:
                e.wait_ge(s.h, v)
            ins = fn(e)
            if inc is not None:
                ins.then_inc(inc.h, n)
        queues[eng].append(do)
        return (inc, inc.c) if inc is not None else None

    def wm(s):
        return (s, s.c)

    stack = contextlib.ExitStack()
    BASE0 = 16512
    assert BASE0 >= nc.sbuf_base

    def sb(name, shape, dtype, lo, hi=None):
        nbytes = shape[1] * (2 if dtype == BF16 else 4)
        if hi is None:
            hi = lo + nbytes
        assert lo % 32 == 0 and hi - lo >= nbytes, (name, lo, hi, nbytes)
        assert BASE0 + hi <= nc.sbuf_top, (name, lo, hi)
        return nc.alloc_sbuf_tensor_at(name, shape, dtype, offset=BASE0 + lo)

    IH = I // 2
    assert (I // 2) % 128 == 0 or I * 4 <= 11 * KB or True
    o_wnat0 = 6 * KB
    o_wnat1 = o_wnat0 + 4 * IH
    o_tqd0 = o_wnat1 + 4 * IH
    o_tqd1 = o_tqd0 + 2048
    o_wqdn = o_tqd1 + 2048
    o_xnst = max(o_wqdn, 6 * KB + 16 * D)
    o_wv = max(o_wqdn + 2 * I, o_xnst + 8 * D)
    o_wv += (-o_wv) % 32
    o_wk = o_wv + 2 * ND * D
    o_vtm = o_wk + 2 * ND * D
    assert o_vtm + 2 * NT * D <= 94 * KB, (o_vtm, "A2a-era overflow")
    o_qf = 94 * KB - 2 * ND * T
    o_ofm = o_qf - 2 * ND * T
    assert o_qf >= o_vtm or o_qf >= o_wk + 2 * ND * D, "qf overlaps live wq"
    assert o_ofm >= o_wv
    o_xrb = 6 * KB
    o_rn = o_xrb + 16 * D
    assert o_rn + 4 * T <= o_ofm

    assert NT * D * 4 <= 64 * KB
    assert ND * T * 2 <= 32 * KB
    assert NI * T * 2 <= 88 * KB
    assert NI * D * 2 <= 44 * KB
    assert ND * D * 2 <= 16 * KB
    assert ND * 129 * 4 <= 4200

    c_ident = sb("ident_s", [128, 128], BF16, 0)
    c_wfbd = sb("wfbd_s", [128, 128], BF16, 256)
    c_mask = sb("mask_s", [128, 128], F32, 512)
    c_ones = sb("ones_s", [128, 1], BF16, 1024, 1088)
    c_bf = sb("bf_s", [128, 1], F32, 1088, 1152)
    c_kvbd = sb("kvbd_s", [128, 128], BF16, 1152)
    c_ksr = sb("ksr_s", [128, 128], BF16, 1408)
    c_wfbdf = sb("wfbdf_s", [128, 128], F32, 1664)
    c_cols = sb("cols_s", [128, 96], F32, 2176)
    c_stats = sb("stats_s", [128, 48], F32, 2560)
    c_clip = sb("clip_s", [1, 2], F32, 2752, 2784)
    c_sig = sb("sig_s", [1, 2], F32, 2784, 2816)
    c_s1 = sb("s1_s", [1, 8], F32, 2816, 2880)
    c_bo = sb("bo_s", [128, D], F32, 202 * KB + 512)
    c_identf = sb("identf_s", [128, 128], F32, 2880)
    assert 2880 + 512 <= 6 * KB

    c_xr = sb("xr", [128, 4 * D], F32, 6 * KB)
    c_hpre = sb("hpre", [128, NI * T], BF16, 6 * KB)
    c_wdst0 = sb("wdst0", [128, IH], F32, o_wnat0)
    c_wdst1 = sb("wdst1", [128, IH], F32, o_wnat1)
    c_tqd0 = sb("tqd0", [128, 512], F32, o_tqd0)
    c_tqd1 = sb("tqd1", [128, 512], F32, o_tqd1)
    c_wqdn = sb("wqdn", [128, I], BF16, o_wqdn)
    c_wv = sb("wv_s", [128, ND * D], BF16, o_wv)
    c_wk = sb("wk_s", [128, ND * D], BF16, o_wk)
    c_wq = sb("wq_s", [128, ND * D], BF16, o_wk)
    c_vtm = sb("vtm", [128, NT * D], BF16, o_vtm)
    c_xnst = [sb(f"xnst{k}", [128, D], BF16, o_xnst + k * 2 * D) for k in range(4)]
    c_qf = sb("qf", [128, ND * T], BF16, o_qf)
    o_sld = o_vtm
    c_wgs = [sb(f"wg{k}s", [128, D], F32, o_sld + k * 4 * D) for k in range(2)]
    c_wus = [sb(f"wu{k}s", [128, D], F32, o_sld + (2 + k) * 4 * D) for k in range(2)]
    c_tqw01 = sb("tqw01", [128, D], F32, o_sld + 16 * D)
    c_ixst2 = sb("ixst2", [128, 512], F32, o_sld + 20 * D)
    c_wqng01 = [sb(f"wqng01_{k}", [128, D], BF16,
                   o_sld + 20 * D + 2048 + k * 4 * D) for k in range(2)]
    c_wqnu01 = [sb(f"wqnu01_{k}", [128, D], BF16,
                   o_sld + 20 * D + 2048 + 2 * D + k * 4 * D) for k in range(2)]
    assert o_sld + 20 * D + 2048 + 8 * D <= 94 * KB
    c_rn2 = sb("rn2", [128, T], F32, 6 * KB)
    assert 6 * KB + 4 * T <= o_xnst
    c_ofm = sb("ofm", [128, ND * T], BF16, o_ofm)
    c_xrb = sb("xrb", [128, 4 * D], F32, o_xrb)
    c_rn = sb("rn", [128, T], F32, o_rn)
    c_xn = sb("xn", [128, ND * T], BF16, 94 * KB)
    c_intx = sb("intx", [128, ND * T], BF16, 94 * KB)
    c_ytile = sb("ytile", [128, 2 * D], F32, 94 * KB)
    c_accrb = sb("accrb", [128, 2 * D], F32, 94 * KB + 8 * D)
    c_ixst = sb("ixst", [128, T], F32, 126 * KB)
    c_yst0 = sb("yst0", [128, D], BF16, 126 * KB)
    c_yst1 = sb("yst1", [128, D], BF16, 126 * KB + 4 * D)
    c_yab = sb("yab", [128, 2 * D], BF16, 94 * KB + 16 * D)
    c_tq = sb("tq", [128, 2048], F32, 126 * KB)
    c_tqw = sb("tqw", [128, 1024], F32, 126 * KB)
    c_wqng = sb("wqng", [128, D], BF16, 130 * KB)
    c_wqnu = sb("wqnu", [128, D], BF16, 132 * KB)
    assert 132 * KB + 2 * D <= 134 * KB
    c_wdqT = sb("wdqT", [128, NI * D], BF16, 134 * KB)
    c_sta = sb("sta", [128, T], BF16, 178 * KB)
    c_stb = sb("stb", [128, T], BF16, 178 * KB + 2 * T)
    c_stc = sb("stc", [128, T], BF16, 178 * KB + 4 * T)
    assert 178 * KB + 6 * T <= 194 * KB
    c_wo = sb("wo_s", [128, ND * D], BF16, 178 * KB)
    c_wgn = [sb("wgn0", [128, D], F32, 178 * KB), sb("wgn1", [128, D], F32, 182 * KB)]
    c_wun = [sb("wun0", [128, D], F32, 186 * KB), sb("wun1", [128, D], F32, 190 * KB)]
    c_kvs = sb("kvs", [128, ND * 129], F32, 194 * KB)
    c_kvr = sb("kvr", [128, ND * 129], F32, 198 * KB + 256)
    c_sg = sb("sg", [128, T], F32, 194 * KB)
    c_slabg = sb("slabg", [128, D], BF16, 202 * KB + 512)
    c_slabu = sb("slabu", [128, D], BF16, 202 * KB + 512 + 2 * D)
    assert 202 * KB + 512 + 4 * D <= 207 * KB

    c_ps = stack.enter_context(nc.psum_tensor("ps", [128, 4096], F32))

    COL = {}

    def col(name):
        if name not in COL:
            COL[name] = len(COL)
            assert len(COL) <= 96
        j = COL[name]
        return c_cols[:, j:j + 1]

    RG = [list(range(cfg.n_cores))]

    E("sync", lambda e: e.dma_start(out=c_ident[:], in_=d_ident[:, :]), inc=dsc, n=16)
    E("sync", lambda e: e.dma_start(out=c_wfbdf[:], in_=d_wfbd[:, :]), inc=dsc, n=16)
    E("sync", lambda e: e.dma_start(out=c_mask[:], in_=d_mask[:, :]), inc=dsc, n=16)
    E("sync", lambda e: e.dma_start(out=c_ones[:], in_=d_ones[:, :]), inc=dsc, n=16)
    E("sync", lambda e: e.dma_start(out=c_bf[:], in_=d_bf[:, :]), inc=dsc, n=16)
    E("sync", lambda e: e.dma_start(out=c_bo[:], in_=d_bo[:, :]), inc=dsc, n=16)
    E("sync", lambda e: e.dma_start(out=c_clip[:, :], in_=d_clip[None, :]), inc=dsc, n=16)
    E("sync", lambda e: e.dma_start(out=c_identf[:], in_=d_identf[:, :]), inc=dsc, n=16)
    ev_consts = wm(dsc)
    ev_ident = ev_mask = ev_ones = ev_bfl = ev_bo = ev_clip = ev_consts
    sigxb, sigdb = col("sigxb"), col("sigdb")
    assert COL["sigdb"] == COL["sigxb"] + 1
    ev_wvld_c = []
    _wv_sems = [dgv, dsp, dwr]
    _wv_prev = [None] * len(_wv_sems)
    for c in range(ND):
        k = c % len(_wv_sems)
        pv = _wv_prev[k]
        ev = E("gpsimd", lambda e, c=c, k=k: e.dma_start(
            out=c_wv[:, c * D:(c + 1) * D], in_=d_wv[c * 128:(c + 1) * 128, :]),
            waits=(pv,) if pv is not None else (), inc=_wv_sems[k], n=16)
        _wv_prev[k] = ev
        ev_wvld_c.append(ev)
    ev_wvtail = [e for e in _wv_prev if e is not None]

    nhalf = (D + 511) // 512
    ev_xn = [None] * NT
    ev_xn_half = [None] * NT
    ev_xnev_a = [None, None]
    ev_vtm = [None] * NT
    ev_tpose = [None] * 4
    ev_xnev = [None, None]
    ev_vev = [None, None]
    ev_apply_slot = [None] * 4
    ev_applyt = [None] * NT
    ev_mxacc = None
    mxa = col("mxacc")

    def emit_LN(tt):
        par = tt % 2
        slot = tt % 4
        xrs = c_xr[:, slot * D:(slot + 1) * D]
        w_x = [ev_apply_slot[slot]] if ev_apply_slot[slot] is not None else []
        ev_xd = E("sync", lambda e, tt=tt, xrs=xrs: e.dma_start(
            out=xrs, in_=d_x[tt * 128:(tt + 1) * 128, :]),
            waits=tuple(w_x), inc=dsx[slot], n=16)
        ev_xd_t[tt] = ev_xd
        stats = c_stats[:, par * 24: par * 24 + 6 * nhalf]
        agg2 = c_stats[:, 36 + par * 2: 38 + par * 2]
        ev_bn = None
        for h in range(nhalf):
            lo, hi = h * 512, min(D, (h + 1) * 512)
            ev_bn = E("vector", lambda e, h=h, lo=lo, hi=hi, stats=stats, xrs=xrs: e.bn_stats(
                stats[:, 6 * h:6 * h + 6], xrs[:, lo:hi]),
                waits=(ev_xd,) if h == 0 else (), inc=(sv if h == nhalf - 1 else None))
        ev_agg = E("vector", lambda e, stats=stats, agg2=agg2: e.bn_aggr(agg2, stats),
                   waits=(ev_bn,), inc=sv)
        veps, s0 = col(f"veps{par}"), col(f"s0{par}")
        rstd, nbias = col(f"rstd{par}"), col(f"nbias{par}")
        ev_veps = E("vector", lambda e, agg2=agg2, veps=veps: e.tensor_scalar(
            veps, agg2[:, 1:2], 1e-5, None, ALU.add), waits=(ev_agg,), inc=sv)
        ev_s0 = E("scalar", lambda e, veps=veps, s0=s0: e.activation(s0, veps, AF.Sqrt),
                  waits=(ev_veps,), inc=ss)
        ev_rstd = E("vector", lambda e, s0=s0, rstd=rstd: e.reciprocal(rstd, s0),
                    waits=(ev_s0,), inc=sv)
        ev_nb = E("vector", lambda e, agg2=agg2, rstd=rstd, nbias=nbias: e.tensor_scalar(
            nbias, agg2[:, 0:1], -1.0, rstd, ALU.mult, ALU.mult), waits=(ev_rstd,), inc=sv)
        xnst = c_xnst[slot]
        w_ap = [ev_nb] + ([ev_tpose[slot]] if ev_tpose[slot] is not None else [])
        ev_apply = E("vector", lambda e, xnst=xnst, rstd=rstd, nbias=nbias, xrs=xrs:
                     e.tensor_scalar(xnst[:], xrs, rstd, nbias, ALU.mult, ALU.add),
                     waits=tuple(w_ap), inc=sv)
        ev_apply_slot[slot] = ev_apply
        ev_applyt[tt] = ev_apply

    def emit_PE_A(tt):
        par = tt % 2
        slot = tt % 4
        xnst = c_xnst[slot]
        pb = 0 if par == 0 else 2
        w_t = [ev_applyt[tt], ev_ident]
        if ev_xnev[par] is not None:
            w_t += [ev_xnev[par], ev_xnev_a[par]]
        ev_tp = None
        for c in range(ND):
            bslot, off = pb + c // 4, (c % 4) * 128
            ev_tp = E("tensor", lambda e, xnst=xnst, c=c, bslot=bslot, off=off: e.matmul(
                c_ps[:, bslot * 512 + off: bslot * 512 + off + 128],
                xnst[:, c * 128:(c + 1) * 128], c_ident[:], start=True, stop=True),
                waits=tuple(w_t) if c == 0 else (), inc=(st if c == ND - 1 else None))
        ev_tpose[slot] = ev_tp
        ev_xn[tt] = E("scalar", lambda e, tt=tt, pb=pb: e.copy(
            c_xn[:, tt * D: tt * D + D], c_ps[:, pb * 512: pb * 512 + D]),
            waits=(ev_tp,), inc=ss)
        ev_xn_half[tt] = ev_xn[tt]
        ev_xnev_a[par] = ev_xn[tt]
        ev_xnev[par] = ev_xn[tt]

    def emit_PE_B(tt):
        par = tt % 2
        b0v = 4 if par == 0 else 6
        w0 = [ev_xn[tt]] + ([ev_wvld_c[0]] if tt < 6 else ev_wvtail)
        if ev_vev[par] is not None:
            w0.append(ev_vev[par])
        ev_vmm = None
        for c in range(ND):
            for h in range(NVH):
                lo, hi = h * 512, min(D, (h + 1) * 512)
                w_c = ()
                if c == 0 and h == 0:
                    w_c = tuple(w0)
                elif tt < 6 and h == 0:
                    w_c = (ev_wvld_c[c],)
                ev_vmm = E("tensor", lambda e, tt=tt, c=c, lo=lo, hi=hi, b0v=b0v: e.matmul(
                    c_ps[:, b0v * 512 + lo: b0v * 512 + hi],
                    c_xn[:, tt * D + c * 128: tt * D + (c + 1) * 128],
                    c_wv[:, c * D + lo: c * D + hi],
                    start=(c == 0), stop=(c == ND - 1)),
                    waits=w_c,
                    inc=(st if (c == ND - 1 and h == NVH - 1) else None))
        ev_vtm[tt] = E("vector", lambda e, tt=tt, b0v=b0v: e.tensor_copy(
            c_vtm[:, tt * D:(tt + 1) * D], c_ps[:, b0v * 512: b0v * 512 + D]),
            waits=(ev_vmm,), inc=sv)
        ev_vev[par] = ev_vtm[tt]

    mxh = col("mxh")
    mxh2 = col("mxh2")
    ev_mh1 = None
    ev_mh2 = None
    ev_wkld = None
    ev_xd_t = {}
    for tt in range(NT):
        if tt >= 1:
            emit_PE_A(tt - 1)
        emit_LN(tt)
        if tt == min(9, NT - 1):
            for c in range(ND):
                ev_wkld = E("gpsimd", lambda e, c=c: e.dma_start(
                    out=c_wk[:, c * D:(c + 1) * D], in_=d_wk[c * 128:(c + 1) * 128, :]),
                    waits=(ev_wkld,) if ev_wkld is not None else (ev_applyt[tt - 1],),
                    inc=dgk, n=16)
        if tt == NT // 2 + 1:
            ev_mh1 = E("vector", lambda e: e.tensor_reduce(
                mxh, c_xn[:, 0:(NT // 2) * D], axis=AX, op=ALU.max,
                apply_absolute_value=True),
                waits=(ev_xn[NT // 2 - 1], ev_xn_half[NT // 2 - 1]), inc=sv)
        if tt == (3 * NT) // 4 + 1 and NT >= 8:
            ev_mh2 = E("vector", lambda e: e.tensor_reduce(
                mxh2, c_xn[:, (NT // 2) * D: (3 * NT // 4) * D], axis=AX,
                op=ALU.max, apply_absolute_value=True),
                waits=(ev_xn[(3 * NT) // 4 - 1], ev_xn_half[(3 * NT) // 4 - 1]),
                inc=sv)
        if tt >= 1:
            emit_PE_B(tt - 1)
    emit_PE_A(NT - 1)
    lo_f = ((3 * NT // 4) if NT >= 8 else (NT // 2)) * D
    ev_mxacc = E("vector", lambda e, lo_f=lo_f: e.tensor_reduce(
        mxa, c_xn[:, lo_f: NT * D], axis=AX, op=ALU.max,
        apply_absolute_value=True),
        waits=(ev_xn[NT - 1], ev_xn_half[NT - 1], ev_mh1), inc=sv)
    ev_mxacc = E("vector", lambda e: e.tensor_tensor(mxa, mxa, mxh, ALU.max),
                 waits=(ev_mxacc,), inc=sv)
    if NT >= 8:
        ev_mxacc = E("vector", lambda e: e.tensor_tensor(mxa, mxa, mxh2, ALU.max),
                     waits=(ev_mxacc, ev_mh2), inc=sv)
    emit_PE_B(NT - 1)
    ev_wfbd = E("vector", lambda e: e.tensor_copy(c_wfbd[:], c_wfbdf[:]),
                waits=(ev_consts,), inc=sv)
    ev_sig = E("scalar", lambda e: e.activation(c_sig[:, :], c_clip[:, :], AF.Sigmoid),
               waits=(ev_consts,), inc=ss)
    ev_s127 = E("vector", lambda e: e.tensor_scalar(
        c_s1[:, 2:4], c_sig[:, :], 1.0 / QMAX, None, ALU.mult), waits=(ev_sig,), inc=sv)
    ev_bsg = E("sync", lambda e: e.dma_start(out=d_bc1[None, :], in_=c_s1[:, 2:4]),
               waits=(ev_s127,), inc=dbs, n=16)
    E("sync", lambda e: e.dma_start(
        out=c_cols[:, COL["sigxb"]:COL["sigxb"] + 2],
        in_=bass.AP(d_bc1, 0, [[0, 128], [1, 2]])), waits=(ev_bsg,), inc=dbs, n=16)
    ev_sig_bc = wm(dbs)
    wm_sv_A = wm(sv)
    wm_ss_A = wm(ss)
    wm_st_A = wm(st)

    ev_r1mm = E("tensor", lambda e: e.matmul(
        c_ps[0:1, 0:128], mxa, c_identf[:], start=True, stop=True),
        waits=(ev_mxacc, ev_consts, wm_ss_A), inc=st)
    gmx1 = c_s1[:, 0:1]
    ev_ar1row = E("vector", lambda e: e.tensor_reduce(
        gmx1, c_ps[0:1, 0:128], axis=AX, op=ALU.max), waits=(ev_r1mm,), inc=sv)
    ev_pre1 = E("vector", lambda e: e.tensor_scalar(
        gmx1, gmx1, c_cols[0:1, COL["sigxb"]:COL["sigxb"] + 1], None, ALU.mult),
        waits=(ev_ar1row, ev_sig_bc), inc=sv)
    ev_c1w = E("sync", lambda e: e.dma_start(out=d_cc1i[None, :], in_=gmx1),
               waits=(ev_pre1,), inc=dt, n=16)
    ev_ar1 = E("gpsimd", lambda e: e.collective_compute(
        "AllReduce", ALU.max, replica_groups=RG,
        ins=[d_cc1i.ap().opt()], outs=[d_cc1o.ap().opt()]), waits=(ev_c1w,), inc=scc)
    rsxb = col("rsxb")
    gmxb = col("gmxb")
    sxb = gmxb
    ev_gmxb = E("gpsimd", lambda e: e.dma_start(
        out=gmxb, in_=bass.AP(d_cc1o, 0, [[0, 128], [1, 1]])),
        waits=(ev_ar1,), inc=dbx, n=16)

    NWSEG = (IH + 511) // 512
    ev_wd_ld = {}
    ev_wd_p1 = {}
    ev_wd_p2 = {}
    ev_tqd_free = [None, None]
    ev_wqdn_read = None
    ev_wd_evac = None

    def emit_wd_load(hc):
        if hc >= ND:
            return
        for half, wnat in ((0, c_wdst0), (1, c_wdst1)):
            w_ld = [wm_sv_A, wm_st_A] + [wm(dx) for dx in dsx]
            if (hc - 1, half) in ev_wd_p1:
                w_ld = [ev_wd_p1[(hc - 1, half)]]
            ev_wd_ld[(hc, half)] = E("sync", lambda e, hc=hc, half=half, wnat=wnat: e.dma_start(
                out=wnat[:], in_=d_wd[hc * 128:(hc + 1) * 128, half * IH:(half + 1) * IH]),
                waits=tuple(w_ld), inc=dsw[half], n=16)

    def emit_wd_quant(hc):
        wmx0, wmx1 = col("wdm0"), col("wdm1")
        r0, rsw, sw = col("wdr0"), col("wdrs"), col("wdsw")
        ev_m0 = E("vector", lambda e, wmx0=wmx0: e.tensor_reduce(
            wmx0, c_wdst0[:], axis=AX, op=ALU.max, apply_absolute_value=True),
            waits=(ev_wd_ld[(hc, 0)],), inc=sv)
        ev_m1 = E("vector", lambda e, wmx1=wmx1: e.tensor_reduce(
            wmx1, c_wdst1[:], axis=AX, op=ALU.max, apply_absolute_value=True),
            waits=(ev_wd_ld[(hc, 1)],), inc=sv)
        ev_mm_ = E("vector", lambda e, wmx0=wmx0, wmx1=wmx1: e.tensor_tensor(
            wmx0, wmx0, wmx1, ALU.max), waits=(ev_m0, ev_m1), inc=sv)
        ev_r0 = E("vector", lambda e, wmx0=wmx0, r0=r0: e.reciprocal(r0, wmx0),
                  waits=(ev_mm_,), inc=sv)
        ev_rsw = E("vector", lambda e, r0=r0, rsw=rsw: e.tensor_scalar(
            rsw, r0, QMAX, None, ALU.mult), waits=(ev_r0,), inc=sv)
        ev_sw = E("vector", lambda e, wmx0=wmx0, sw=sw: e.tensor_scalar(
            sw, wmx0, 1.0 / QMAX, None, ALU.mult), waits=(ev_rsw,), inc=sv)
        p2s = []
        ev_p1_last = [None, None]
        segi = 0
        for half, wnat in ((0, c_wdst0), (1, c_wdst1)):
            for sg_ in range(NWSEG):
                lo, hi = sg_ * 512, min(IH, (sg_ + 1) * 512)
                tqd = c_tqd0 if segi % 2 == 0 else c_tqd1
                w_p1 = [ev_sw]
                if ev_tqd_free[segi % 2] is not None:
                    w_p1.append(ev_tqd_free[segi % 2])
                ev_p1 = E("scalar", lambda e, wnat=wnat, lo=lo, hi=hi, tqd=tqd, rsw=rsw:
                          e.activation(tqd[:, 0:hi - lo], wnat[:, lo:hi],
                                       AF.Copy, bias=MAGIC, scale=rsw),
                          waits=tuple(w_p1), inc=ss)
                ev_p1_last[half] = ev_p1
                w_p2 = [ev_p1] + ([ev_wqdn_read] if ev_wqdn_read is not None else [])
                ev_p2 = E("vector", lambda e, half=half, lo=lo, hi=hi, tqd=tqd, sw=sw:
                          e.tensor_scalar(c_wqdn[:, half * IH + lo: half * IH + hi],
                                          tqd[:, 0:hi - lo], MAGIC, sw,
                                          ALU.subtract, ALU.mult),
                          waits=tuple(w_p2), inc=sv)
                ev_tqd_free[segi % 2] = ev_p2
                p2s.append(((half * IH + lo) // 128, (half * IH + hi + 127) // 128, ev_p2))
                segi += 1
        ev_wd_p1[(hc, 0)] = ev_p1_last[0]
        ev_wd_p1[(hc, 1)] = ev_p1_last[1]
        ev_wd_p2[hc] = p2s

    def p2_cover(hc, ic):
        best = None
        for lo_t, hi_t, ev in ev_wd_p2[hc]:
            if lo_t < ic + 1 and hi_t > ic:
                if best is None or ev[1] > best[1]:
                    best = ev
        return best if best is not None else ev_wd_p2[hc][-1][2]

    emit_wd_load(0)
    prev = None
    ev_kv_all = None
    ev_kmm_last = None
    for hp in range(ND):
        w_k = [ev_wkld]
        if prev is None:
            w_k += [ev_ar1row, wm_ss_A]
        else:
            w_k += [prev["tr_ev"], prev["ks_ev"]]
        ev_kmm = None
        for c in range(ND):
            for j in range(NTC):
                ev_kmm = E("tensor", lambda e, hp=hp, c=c, j=j: e.matmul(
                    c_ps[:, j * 512:(j + 1) * 512],
                    c_wk[:, c * D + hp * 128: c * D + hp * 128 + 128],
                    bass.AP(c_xn, (4 * j) * D + c * 128, [[ND * T, 128], [D, 4], [1, 128]]),
                    start=(c == 0), stop=(c == ND - 1)),
                    waits=tuple(w_k) if (c == 0 and j == 0) else (),
                    inc=(st if (c == ND - 1 and j == NTC - 1) else None))
        ev_kmm_last = ev_kmm
        w_kev = [ev_kmm] + ([prev["kf_mm"]] if prev is not None else [])
        ev_kev_a = E("scalar", lambda e: e.copy(c_sta[:, 0:T // 2], c_ps[:, 0:T // 2]),
                     waits=tuple(w_kev), inc=ss)
        ev_kev = E("scalar", lambda e: e.copy(c_sta[:, T // 2:T], c_ps[:, T // 2:T]),
                   waits=(), inc=ss)
        ev_kfmm = ev_kfmm_h1 = None
        for j in range(NTC):
            w_kf = [ev_kev_a if j < NTC // 2 else ev_kev, ev_wfbd]
            if j == 0:
                if prev is not None:
                    w_kf += [prev["ks_ev"], prev["wd_ev"]]
                else:
                    w_kf += [ev_vtm[NT - 1], ev_vtm[NT - 2] if NT >= 2 else ev_vtm[NT - 1]]
            ev_kfmm = E("tensor", lambda e, j=j: e.matmul(
                c_ps[:, (4 + j) * 512:(5 + j) * 512], c_wfbd[:],
                c_sta[:, j * 512:(j + 1) * 512], start=True, stop=True),
                waits=tuple(w_kf) if j in (0, NTC // 2) else (),
                inc=(st if j in (NTC // 2 - 1, NTC - 1) else None))
            if j == NTC // 2 - 1:
                ev_kfmm_h1 = ev_kfmm
        w_kfe = [ev_kfmm_h1, ev_bfl] + ([prev["tr_mm"]] if prev is not None else [])
        ev_kfev_a = E("scalar", lambda e: e.activation(
            c_stb[:, 0:T // 2], c_ps[:, 4 * 512: 4 * 512 + T // 2],
            AF.Relu, bias=c_bf[:], scale=1.0), waits=tuple(w_kfe), inc=ss)
        ev_kfev = E("scalar", lambda e: e.activation(
            c_stb[:, T // 2:T], c_ps[:, 4 * 512 + T // 2: 4 * 512 + T],
            AF.Relu, bias=c_bf[:], scale=1.0), waits=(ev_kfmm,), inc=ss)
        ev_trmm = ev_trmm_h1 = None
        for tt in range(NT):
            bslot, off = tt // 4, (tt % 4) * 128
            w_t = [ev_kfev_a if tt < NT // 2 else ev_kfev]
            if prev is not None and tt == 0:
                w_t.append(prev["tr_ev"])
            ev_trmm = E("tensor", lambda e, tt=tt, bslot=bslot, off=off: e.matmul(
                c_ps[:, bslot * 512 + off: bslot * 512 + off + 128],
                c_stb[:, tt * 128:(tt + 1) * 128], c_ident[:], start=True, stop=True),
                waits=tuple(w_t) if tt in (0, NT // 2) else (),
                inc=(st if tt in (NT // 2 - 1, NT - 1) else None))
            if tt == NT // 2 - 1:
                ev_trmm_h1 = ev_trmm
        w_tre = [ev_trmm_h1] + ([prev["kv_mm"]] if prev is not None else [])
        ev_trev_a = E("scalar", lambda e: e.copy(c_stc[:, 0:T // 2], c_ps[:, 0:T // 2]),
                      waits=tuple(w_tre), inc=ss)
        ev_trev = E("scalar", lambda e: e.copy(c_stc[:, T // 2:T], c_ps[:, T // 2:T]),
                    waits=(ev_trmm,), inc=ss)
        emit_wd_quant(hp)
        emit_wd_load(hp + 1)
        ev_kvmm = None
        for tt in range(NT):
            w_kv = [ev_trev_a, ev_ones, ev_vtm[tt]]
            if prev is not None and tt == 0:
                w_kv.append(prev["ks_ev"])
            if tt == NT // 2:
                w_kv = [ev_vtm[tt], ev_trev]
            E("tensor", lambda e, tt=tt, hp=hp: e.matmul(
                c_ps[:, 4 * 512: 4 * 512 + 128], c_stc[:, tt * 128:(tt + 1) * 128],
                c_vtm[:, tt * D + hp * 128: tt * D + hp * 128 + 128],
                start=(tt == 0), stop=(tt == NT - 1)),
                waits=tuple(w_kv) if (tt == 0 or tt == NT // 2) else (ev_vtm[tt],))
            ev_kvmm = E("tensor", lambda e, tt=tt: e.matmul(
                c_ps[:, 5 * 512: 5 * 512 + 1], c_stc[:, tt * 128:(tt + 1) * 128], c_ones[:],
                start=(tt == 0), stop=(tt == NT - 1)),
                inc=(st if tt == NT - 1 else None))
        ev_kvev = E("vector", lambda e, hp=hp: e.tensor_copy(
            c_kvs[:, hp * 129: hp * 129 + 128], c_ps[:, 4 * 512: 4 * 512 + 128]),
            waits=(ev_kvmm,), inc=sv)
        ev_ksev = E("vector", lambda e, hp=hp: e.tensor_copy(
            c_kvs[:, hp * 129 + 128: hp * 129 + 129], c_ps[:, 5 * 512: 5 * 512 + 1]),
            waits=(ev_kvev,), inc=sv)
        ngrp = (NI + 7) // 8
        ev_wd_ev_hp = None
        for g in range(ngrp):
            ic0, ic1 = g * 8, min(NI, g * 8 + 8)
            ev_tl = None
            for ic in range(ic0, ic1):
                slot_ = ic - ic0
                bslot, off = 6 + slot_ // 4, (slot_ % 4) * 128
                w_t = [p2_cover(hp, ic)]
                if ic == ic0:
                    w_t += [ev_ident, ev_kfev]
                    if g > 0:
                        w_t.append(ev_wd_ev_hp)
                    elif prev is not None:
                        w_t.append(prev["wd_ev"])
                ev_tl = E("tensor", lambda e, ic=ic, hp=hp, bslot=bslot, off=off: e.matmul(
                    c_ps[:, bslot * 512 + off: bslot * 512 + off + 128],
                    c_wqdn[:, ic * 128:(ic + 1) * 128], c_ident[:], start=True, stop=True),
                    waits=tuple(w_t),
                    inc=(st if ic == ic1 - 1 else None))
            cnt = ic1 - ic0

            def wdevac(e, ic0=ic0, cnt=cnt, hc=hp):
                src = c_ps[:, 6 * 512: 6 * 512 + cnt * 128]
                dst = bass.AP(c_wdqT, ic0 * D + hc * 128, [[NI * D, 128], [D, cnt], [1, 128]])
                return e.copy(dst, src)
            ev_wd_ev_hp = E("scalar", lambda e, f=wdevac: f(e), waits=(ev_tl,), inc=ss)
        ev_wqdn_read = ev_tl if False else ev_tl
        ev_wd_evac = ev_wd_ev_hp
        prev = {"tr_ev": ev_trev, "tr_mm": ev_trmm, "ks_ev": ev_ksev, "kf_ev": ev_kfev,
                "kf_mm": ev_kfmm, "kv_mm": ev_kvmm, "wd_ev": ev_wd_ev_hp}
        ev_kv_all = ev_ksev
    ev_wd_done = ev_wd_evac

    ev_wqld = None
    for c in range(ND):
        ev_wqld = E("gpsimd", lambda e, c=c: e.dma_start(
            out=c_wq[:, c * D:(c + 1) * D], in_=d_wq[c * 128:(c + 1) * 128, :]),
            waits=(ev_kmm_last,) if c == 0 else (), inc=dgq, n=16)

    ev_c2w = E("sync", lambda e: e.dma_start(out=d_cc2i[:, :], in_=c_kvs[:]),
               waits=(ev_kv_all, wm(dt)), inc=dt, n=16)
    ev_ar2 = E("gpsimd", lambda e: e.collective_compute(
        "AllReduce", ALU.add, replica_groups=RG,
        ins=[d_cc2i.ap().opt()], outs=[d_cc2o.ap().opt()]), waits=(ev_c2w,), inc=scc)
    ev_c2r = E("sync", lambda e: e.dma_start(out=c_kvr[:], in_=d_cc2o[:, :]),
               waits=(ev_ar2, wm(dt)), inc=dt, n=16)

    prevq = None
    ev_qf = [None] * ND
    for hp in range(ND):
        w_q = [ev_wqld]
        if prevq is None:
            w_q += [prev["tr_ev"], prev["ks_ev"]]
        else:
            w_q += [prevq["qf_mm"], prevq["q_ev"]]
        ev_qmm = None
        for c in range(ND):
            for j in range(NTC):
                ev_qmm = E("tensor", lambda e, hp=hp, c=c, j=j: e.matmul(
                    c_ps[:, j * 512:(j + 1) * 512],
                    c_wq[:, c * D + hp * 128: c * D + hp * 128 + 128],
                    bass.AP(c_xn, (4 * j) * D + c * 128, [[ND * T, 128], [D, 4], [1, 128]]),
                    start=(c == 0), stop=(c == ND - 1)),
                    waits=tuple(w_q) if (c == 0 and j == 0) else (),
                    inc=(st if (c == ND - 1 and j == NTC - 1) else None))
        ev_qev_a = E("vector", lambda e: e.tensor_copy(
            c_sta[:, 0:T // 2], c_ps[:, 0:T // 2]), waits=(ev_qmm,), inc=sv)
        ev_qev = E("vector", lambda e: e.tensor_copy(
            c_sta[:, T // 2:T], c_ps[:, T // 2:T]), waits=(ev_qev_a,), inc=sv)
        ev_qfmm = None
        for j in range(NTC):
            w_qf = [ev_qev_a if j < NTC // 2 else ev_qev]
            if j == 0:
                if prevq is not None:
                    w_qf.append(prevq["qf_ev"])
                else:
                    w_qf += [prev["ks_ev"], prev["wd_ev"]]
            ev_qfmm = E("tensor", lambda e, j=j: e.matmul(
                c_ps[:, (4 + j) * 512:(5 + j) * 512], c_wfbd[:],
                c_sta[:, j * 512:(j + 1) * 512], start=True, stop=True),
                waits=tuple(w_qf) if j in (0, NTC // 2) else (),
                inc=(st if j == NTC - 1 else None))
        w_qfe = [ev_qfmm, ev_bfl] + ([ev_kv_all] if hp == 0 else [])
        ev_qf[hp] = E("scalar", lambda e, hp=hp: e.activation(
            c_qf[:, hp * T:(hp + 1) * T], c_ps[:, 4 * 512: 4 * 512 + T],
            AF.Relu, bias=c_bf[:], scale=1.0), waits=tuple(w_qfe), inc=ss)
        prevq = {"q_ev": ev_qev, "qf_ev": ev_qf[hp], "qf_mm": ev_qfmm}

    ev_rsxv = E("vector", lambda e: e.reciprocal(rsxb, gmxb), waits=(ev_gmxb,), inc=sv)
    ev_sxb = ev_rsxb = ev_rsxv

    wm_sv_A2 = wm(sv)
    _intx_state = {"prev": None, "q": [None] * NTC}

    def emit_intx_chunk(c, q, scratch=None, w_extra=()):
        QW = T // NTC
        lo = q * QW
        st_ = c_ixst[:, lo:lo + QW] if scratch is None else scratch[:, 0:QW]
        w_p1 = [ev_rsxb] + list(w_extra)
        if c == 0 and q == 0:
            w_p1 += [prevq["qf_mm"], ev_qf[ND - 1]]
        if scratch is None and _intx_state["q"][q] is not None:
            w_p1.append(_intx_state["q"][q])
        if scratch is not None and _intx_state["prev"] is not None:
            w_p1.append(_intx_state["prev"])
        ev_p1 = E("vector", lambda e, c=c, lo=lo, QW=QW, st_=st_: e.tensor_scalar(
            st_, c_xn[:, c * T + lo: c * T + lo + QW],
            rsxb, MAGIC, ALU.mult, ALU.add), waits=tuple(w_p1), inc=sv)
        ev_p2 = E("vector", lambda e, QW=QW, st_=st_: e.tensor_scalar(
            st_, st_, MAGIC, -QMAX - 1.0, ALU.subtract, ALU.max), waits=(ev_p1,), inc=sv)
        ev_p3 = E("vector", lambda e, c=c, lo=lo, QW=QW, st_=st_: e.tensor_scalar(
            c_intx[:, c * T + lo: c * T + lo + QW], st_,
            QMAX, None, ALU.min), waits=(ev_p2,), inc=sv)
        if scratch is None:
            _intx_state["q"][q] = ev_p3
        _intx_state["prev"] = ev_p3

    ev_wold = None
    for c in range(ND):
        ev_wold = E("gpsimd", lambda e, c=c: e.dma_start(
            out=c_wo[:, c * D:(c + 1) * D], in_=d_wo[c * 128:(c + 1) * 128, :]),
            waits=(prevq["qf_mm"],) if c == 0 else (), inc=dgo, n=16)

    prevC = None
    ev_ofm = {}
    ev_rn_w = {}

    def emit_ksr(hp, w_ksr):
        ksum_col = c_kvr[:, hp * 129 + 128: hp * 129 + 129]
        return E("scalar", lambda e, ksum_col=ksum_col: e.activation(
            c_ksr[:], c_mask[:], AF.Copy, bias=0.0, scale=ksum_col),
            waits=tuple(w_ksr), inc=ss)

    def emit_kvbd(hp, w_kvbd):
        return E("vector", lambda e, hp=hp: e.tensor_tensor(
            c_kvbd[:], c_kvr[:, hp * 129: hp * 129 + 128], c_mask[:], ALU.mult),
            waits=tuple(w_kvbd), inc=sv)

    cur_ksr = emit_ksr(0, [ev_c2r, ev_mask])
    cur_kvbd = emit_kvbd(0, [ev_c2r, ev_mask])
    ev_rn_sv = {}
    ev_multd = {}

    def rn_of(hp):
        return c_rn if hp % 2 == 0 else c_rn2

    def emit_mult(hp, j):
        rb = rn_of(hp)
        ev_multd[(hp, j)] = E("vector", lambda e, hp=hp, j=j, rb=rb: e.tensor_tensor(
            c_ofm[:, hp * T + j * 512: hp * T + (j + 1) * 512],
            c_ofm[:, hp * T + j * 512: hp * T + (j + 1) * 512],
            rb[:, j * 512:(j + 1) * 512], ALU.mult),
            waits=(ev_rn_sv[(hp, j)], ev_ofm[(hp, j)]), inc=sv)

    for hp in range(ND):
        ev_ksr, ev_kvbd = cur_ksr, cur_kvbd
        rb = rn_of(hp)
        ev_rn = {}
        ev_rnm_d = {}
        for j in range(NTC):
            bslot = j % 2
            w_n = [ev_ksr, ev_qf[hp]]
            if prevC is None and j < 2:
                w_n += [prevq["q_ev"]]
            if prevC is not None and j < 2:
                w_n.append(prevC["rnm"][min(2 + (j % 2), NTC - 1)])
            if j >= 2:
                w_n.append(ev_rnm_d[j - 2])
            ev_nmm = E("tensor", lambda e, hp=hp, j=j, bslot=bslot: e.matmul(
                c_ps[:, bslot * 512:(bslot + 1) * 512], c_ksr[:],
                c_qf[:, hp * T + j * 512: hp * T + (j + 1) * 512], start=True, stop=True),
                waits=tuple(w_n), inc=st)
            w_rnm = [ev_nmm]
            if (hp - 2, j) in ev_multd:
                w_rnm.append(ev_multd[(hp - 2, j)])
            ev_rnm = E("vector", lambda e, j=j, bslot=bslot, rb=rb: e.tensor_scalar(
                rb[:, j * 512:(j + 1) * 512], c_ps[:, bslot * 512:(bslot + 1) * 512],
                1e-6, None, ALU.max), waits=tuple(w_rnm), inc=sv)
            ev_rnm_d[j] = ev_rnm
            if hp >= 1:
                emit_mult(hp - 1, j)

            def emit_lnexp(j, ev_rnm):
                ev_rln = E("scalar", lambda e, j=j, rb=rb: e.activation(
                    rb[:, j * 512:(j + 1) * 512], rb[:, j * 512:(j + 1) * 512],
                    AF.Ln), waits=(ev_rnm,), inc=ss)
                ev_rn_sv[(hp, j)] = E("scalar", lambda e, j=j, rb=rb: e.activation(
                    rb[:, j * 512:(j + 1) * 512], rb[:, j * 512:(j + 1) * 512],
                    AF.Exp, scale=-1.0), waits=(ev_rln,), inc=ss)

            if j < NTC - 1:
                emit_lnexp(j, ev_rnm)
        if hp + 1 < ND:
            cur_ksr = emit_ksr(hp + 1, [ev_nmm])
        emit_lnexp(NTC - 1, ev_rnm_d[NTC - 1])
        ev_o_last = None
        for j in range(NTC):
            bslot = 2 + j % 2
            w_o = [ev_kvbd, ev_qf[hp]]
            if prevC is not None and j < 2:
                w_o.append(prevC["o_last"])
            if j >= 2:
                w_o.append(ev_ofm[(hp, j - 2)])
            if prevC is None and hp == 0 and j == 0:
                w_o += [prevq["qf_mm"], ev_c2w]
            ev_omm = E("tensor", lambda e, hp=hp, j=j, bslot=bslot: e.matmul(
                c_ps[:, bslot * 512:(bslot + 1) * 512], c_kvbd[:],
                c_qf[:, hp * T + j * 512: hp * T + (j + 1) * 512], start=True, stop=True),
                waits=tuple(w_o), inc=st)
            ev_ofm[(hp, j)] = E("vector", lambda e, hp=hp, j=j, bslot=bslot: e.tensor_copy(
                c_ofm[:, hp * T + j * 512: hp * T + (j + 1) * 512],
                c_ps[:, bslot * 512:(bslot + 1) * 512]), waits=(ev_omm,), inc=sv)
            ev_o_last = ev_ofm[(hp, j)]
        prevC = {"out_mm": ev_omm, "o_last": ev_o_last, "rnm": ev_rnm_d}
        if hp + 1 < ND:
            cur_kvbd = emit_kvbd(hp + 1, [ev_omm])
    for j in range(NTC):
        emit_mult(ND - 1, j)
    ev_mult_last = ev_multd[(ND - 1, NTC - 1)]
    ev_intx_hp = _intx_state["prev"]
    cs_def = list(range(ND))
    intx_jobs = [(c, q) for c in cs_def for q in range(NTC)]
    intx_done_n = [0]
    wm_sv_ofm = wm(sv)

    wm_st_hpC = wm(st)
    ev_sld = {}
    prev_g, prev_u = ev_wold, None
    for k in range(2):
        ev_sld[("g", k)] = prev_g = E(
            "gpsimd", lambda e, k=k: e.dma_start(
                out=c_wgs[k][:], in_=d_wg[k * 128:(k + 1) * 128, :]),
            waits=(wm_st_hpC, prev_g), inc=dgo, n=16)
        ev_sld[("u", k)] = prev_u = E(
            "gpsimd", lambda e, k=k: e.dma_start(
                out=c_wus[k][:], in_=d_wu[k * 128:(k + 1) * 128, :]),
            waits=(wm_st_hpC,) + ((prev_u,) if prev_u is not None else ()),
            inc=dwr, n=16)

    ev_mh = None
    evB = None
    ev_wld_free = {}
    ev_tqw_free = None
    ev_wqn_read = None
    ev_slab_gread = None
    ev_slab_uread = None
    ev_quant = {}

    def emit_wload(jc):
        if jc >= NI:
            return
        if jc < 2:
            for wname in ("g", "u"):
                ev_quant[(wname, "ld", jc)] = ev_sld[(wname, jc)]
            return
        par = jc % 2
        for wname, dsrc, bufs, dsems in (("g", d_wg, c_wgn, dswg), ("u", d_wu, c_wun, dswu)):
            wnat = bufs[par]
            w_ld = [ev_wld_free[(wname, jc - 2)]]
            if jc < 4:
                w_ld.append(wm_st_C)
            ev_quant[(wname, "ld", jc)] = E(
                "sync", lambda e, jc=jc, dsrc=dsrc, wnat=wnat: e.dma_start(
                    out=wnat[:], in_=dsrc[jc * 128:(jc + 1) * 128, :]),
                waits=tuple(w_ld), inc=dsems[par], n=16)

    def emit_quant(jc):
        if ("g", jc) in ev_quant:
            return
        for wname in ("g", "u"):
            par = jc % 2
            if jc < 2:
                wnat = (c_wgs if wname == "g" else c_wus)[jc]
                tqw = c_tqw01[:, 0:D]
            else:
                wnat = (c_wgn if wname == "g" else c_wun)[par]
                tqw = c_tqw[:, 0:D]
            if jc < 2:
                wqn = (c_wqng01 if wname == "g" else c_wqnu01)[jc]
            else:
                wqn = c_wqng if wname == "g" else c_wqnu
            wmx = col(f"w{wname}m{par}")
            r0 = col(f"w{wname}r0{par}")
            rsw = col(f"w{wname}rs{par}")
            sw = col(f"w{wname}sw{par}")
            ev_wm = E("vector", lambda e, wnat=wnat, wmx=wmx: e.tensor_reduce(
                wmx, wnat[:], axis=AX, op=ALU.max, apply_absolute_value=True),
                waits=(ev_quant[(wname, "ld", jc)],), inc=sv)
            ev_r0 = E("vector", lambda e, wmx=wmx, r0=r0: e.reciprocal(r0, wmx),
                      waits=(ev_wm,), inc=sv)
            ev_rsw = E("vector", lambda e, r0=r0, rsw=rsw: e.tensor_scalar(
                rsw, r0, QMAX, None, ALU.mult), waits=(ev_r0,), inc=sv)
            ev_sw = E("vector", lambda e, wmx=wmx, sw=sw: e.tensor_scalar(
                sw, wmx, 1.0 / QMAX, None, ALU.mult), waits=(ev_rsw,), inc=sv)
            w_p1 = [ev_sw]
            if jc >= 2:
                if ev_tqw_free2[0] is not None:
                    w_p1.append(ev_tqw_free2[0])
                if jc == 2 and wname == "g":
                    w_p1 += [ev_intx_done, wm_dsy0]
            ev_p1 = E("vector", lambda e, wnat=wnat, rsw=rsw, tqw=tqw: e.tensor_scalar(
                tqw, wnat[:], rsw, MAGIC, ALU.mult, ALU.add),
                waits=tuple(w_p1), inc=sv)
            ev_wld_free[(wname, jc)] = ev_p1
            w_p2 = [ev_p1] + ([ev_wqn_read2[0]] if ev_wqn_read2[0] is not None else [])
            if jc == 2:
                w_p2.append(wm_dsy1)
            ev_p2 = E("vector", lambda e, wqn=wqn, sw=sw, tqw=tqw: e.tensor_scalar(
                wqn[:], tqw, MAGIC, sw, ALU.subtract, ALU.mult),
                waits=tuple(w_p2), inc=sv)
            if jc >= 2:
                ev_tqw_free2[0] = ev_p2
            ev_quant[(wname, jc)] = ev_p2

    ev_tqw_free2 = [None]
    ev_wqn_read2 = [None]

    ev_ymm = [None, None]
    ev_yst = [None, None]
    ev_yst4 = [None] * 4
    ev_ye_b03 = None
    ev_ytw = [None] * NT
    for tt in range(NT):
        par = tt % 2
        slot = tt % 4
        b0 = slot * 2
        ev_amm = None
        for hp in range(ND):
            for h in range(NVH):
                lo, hi = h * 512, min(D, (h + 1) * 512)
                w_a = []
                if hp == 0 and h == 0:
                    w_a = [ev_wold, ev_mult_last]
                    if ev_yst4[slot] is not None:
                        w_a.append(ev_yst4[slot])
                    elif slot >= 2:
                        w_a.append(ev_qf[ND - 1])
                ev_amm = E("tensor", lambda e, tt=tt, hp=hp, lo=lo, hi=hi, b0=b0: e.matmul(
                    c_ps[:, b0 * 512 + lo: b0 * 512 + hi],
                    c_ofm[:, hp * T + tt * 128: hp * T + tt * 128 + 128],
                    c_wo[:, hp * D + lo: hp * D + hi],
                    start=(hp == 0), stop=(hp == ND - 1)),
                    waits=tuple(w_a) if (hp == 0 and h == 0) else (),
                    inc=(st if (hp == ND - 1 and h == NVH - 1) else None))
        ev_ymm[par] = ev_amm
        yst = c_yst0 if par == 0 else c_yst1
        w_ev = [ev_amm]
        if tt < 2 and ev_intx_hp is not None:
            w_ev.append(ev_intx_hp)
        if tt >= 2 and ev_ytw[tt - 2] is not None:
            w_ev.append(ev_ytw[tt - 2])
        ev_ye = E("scalar", lambda e, yst=yst, b0=b0: e.copy(
            yst[:], c_ps[:, b0 * 512: b0 * 512 + D]), waits=tuple(w_ev), inc=ss)
        ev_yst[par] = ev_ye
        ev_yst4[slot] = ev_ye
        if tt == NT - 3:
            ev_ye_b03 = ev_ye
        ev_ytw[tt] = E("sync", lambda e, tt=tt, yst=yst: e.dma_start(
            out=d_ya[tt * 128:(tt + 1) * 128, :], in_=yst[:]), waits=(ev_ye,), inc=dsy[par], n=16)
        if tt == 2:
            emit_wload(0)
            emit_wload(1)
        if tt in (3, 5):
            emit_quant(0 if tt == 3 else 1)
        if tt >= 1:
            n_do = (len(intx_jobs) * min(tt, NT - 4)) // (NT - 4) - intx_done_n[0]
            for _ in range(n_do):
                cdef, q = intx_jobs[intx_done_n[0]]
                emit_intx_chunk(cdef, q, scratch=c_ixst2,
                                w_extra=(wm_st_hpC,) if intx_done_n[0] == 0 else ())
                intx_done_n[0] += 1
    ev_intx_done = _intx_state["prev"]
    ev_ye_last = ev_yst[(NT - 1) % 2]
    wm_st_C = wm(st)
    wm_ss_C = wm(ss)
    wm_sv_C = wm(sv)
    wm_dsy0 = wm(dsya)
    wm_dsy1 = wm(dsyb)

    wm_sgp_C = wm(sgp)

    mha = col("mhacc")
    def emit_slab(ic, ev_silu_prev):
        ev_trB = None
        for wname, base in (("g", 0), ("u", 1024)):
            if ic < 2:
                wqn = (c_wqng01 if wname == "g" else c_wqnu01)[ic]
            else:
                wqn = c_wqng if wname == "g" else c_wqnu
            for c in range(ND):
                w_t = [ev_quant[(wname, ic)], ev_ident]
                if wname == "g" and c == 0:
                    if ev_silu_prev is not None:
                        w_t.append(ev_silu_prev)
                    else:
                        w_t += [ev_mult_last, ev_ye_b03]
                ev_trB = E("tensor", lambda e, wqn=wqn, c=c, base=base: e.matmul(
                    c_ps[:, base + c * 128: base + c * 128 + 128],
                    wqn[:, c * 128:(c + 1) * 128], c_ident[:], start=True, stop=True),
                    waits=tuple(w_t) if c == 0 else (),
                    inc=(st if (wname == "u" and c == ND - 1) else None))
        ev_wqn_read2[0] = ev_trB
        w_se = [ev_trB] + ([ev_slab_gread] if ev_slab_gread is not None else [])
        ev_sgev = E("scalar", lambda e: e.copy(
            c_slabg[:, 0:ND * 128], c_ps[:, 0:ND * 128]),
            waits=tuple(w_se) + ((ev_ye_last,) if ic == 0 else ()), inc=ss)
        w_su = [ev_trB] + ([ev_slab_uread] if ev_slab_uread is not None else [])
        ev_suev = E("vector", lambda e: e.tensor_copy(
            c_slabu[:, 0:ND * 128], c_ps[:, 1024:1024 + ND * 128]),
            waits=tuple(w_su) + ((ev_ye_last,) if ic == 0 else ()), inc=sv)
        return ev_sgev, ev_suev

    emit_wload(0)
    emit_wload(1)
    emit_quant(0)
    cur_sgev, cur_suev = emit_slab(0, None)
    for ic in range(NI):
        emit_wload(ic + 2)
        ev_sgev, ev_suev = cur_sgev, cur_suev
        ev_gmm = None
        for c in range(ND):
            for j in range(NTC):
                w_g = []
                if c == 0 and j == 0:
                    w_g = [ev_sgev, ev_suev, ev_intx_done]
                ev_gmm = E("tensor", lambda e, c=c, j=j: e.matmul(
                    c_ps[:, j * 512:(j + 1) * 512], c_slabg[:, c * 128:(c + 1) * 128],
                    bass.AP(c_intx, (4 * j) * D + c * 128, [[ND * T, 128], [D, 4], [1, 128]]),
                    start=(c == 0), stop=(c == ND - 1)),
                    waits=tuple(w_g) if (c == 0 and j == 0) else (),
                    inc=(st if (c == ND - 1 and j == NTC - 1) else None))
        ev_umm = None
        for c in range(ND):
            for j in range(NTC):
                w_u = []
                if c == 0 and j == 0:
                    w_u = [ev_suev]
                    if evB is not None:
                        w_u.append(evB["stt_ev"])
                    else:
                        w_u += [wm_sv_C, wm_ss_C]
                ev_umm = E("tensor", lambda e, c=c, j=j: e.matmul(
                    c_ps[:, (4 + j) * 512:(5 + j) * 512], c_slabu[:, c * 128:(c + 1) * 128],
                    bass.AP(c_intx, (4 * j) * D + c * 128, [[ND * T, 128], [D, 4], [1, 128]]),
                    start=(c == 0), stop=(c == ND - 1)),
                    waits=tuple(w_u) if (c == 0 and j == 0) else (),
                    inc=(st if (c == ND - 1 and j == NTC - 1) else None))
        ev_slab_gread, ev_slab_uread = ev_gmm, ev_umm
        if ic + 1 < NI:
            emit_quant(ic + 1)
        w_si = [ev_gmm, ev_sxb]
        if evB is not None:
            w_si.append(evB["stt_ev"])
        else:
            w_si += [wm_ss_C, wm_sv_C, wm(dt)]
        ev_silu = E("scalar", lambda e: e.activation(
            c_sg[:], c_ps[:, 0:T], AF.Silu, scale=sxb), waits=tuple(w_si), inc=ss)
        if ic + 1 < NI:
            cur_sgev, cur_suev = emit_slab(ic + 1, ev_silu)
        w_st_ = [ev_silu, ev_umm]
        if ic == 0:
            w_st_ += [wm_dsy0, wm_dsy1]
            w_st_ += [wm_st_C, wm_sv_C]
        hm = col("hm")
        if ic < NI - 1:
            ev_stt = E("vector", lambda e, ic=ic: e.scalar_tensor_tensor(
                c_hpre[:, ic * T:(ic + 1) * T], c_ps[:, 4 * 512: 4 * 512 + T], sxb, c_sg[:],
                ALU.mult, ALU.mult), waits=tuple(w_st_), inc=sv)
            ev_hmr = E("vector", lambda e, ic=ic, hm=hm: e.tensor_reduce(
                hm, c_hpre[:, ic * T:(ic + 1) * T], axis=AX, op=ALU.max,
                apply_absolute_value=True), waits=(ev_stt,), inc=sv)
            if ic == 0:
                ev_mh = E("vector", lambda e, hm=hm: e.tensor_copy(mha, hm),
                          waits=(ev_hmr,), inc=sv)
            else:
                ev_mh = E("vector", lambda e, hm=hm: e.tensor_tensor(
                    mha, mha, hm, ALU.max), waits=(ev_hmr, ev_mh), inc=sv)
        else:
            hm2 = col("hm2")
            ev_stt_a = E("vector", lambda e, ic=ic: e.scalar_tensor_tensor(
                c_hpre[:, ic * T: ic * T + T // 2],
                c_ps[:, 4 * 512: 4 * 512 + T // 2], sxb, c_sg[:, 0:T // 2],
                ALU.mult, ALU.mult), waits=tuple(w_st_), inc=sv)
            ev_hm_a = E("vector", lambda e, ic=ic, hm=hm: e.tensor_reduce(
                hm, c_hpre[:, ic * T: ic * T + T // 2], axis=AX, op=ALU.max,
                apply_absolute_value=True), waits=(ev_stt_a,), inc=sv)
            ev_stt = E("vector", lambda e, ic=ic: e.scalar_tensor_tensor(
                c_hpre[:, ic * T + T // 2:(ic + 1) * T],
                c_ps[:, 4 * 512 + T // 2: 4 * 512 + T], sxb, c_sg[:, T // 2:T],
                ALU.mult, ALU.mult), waits=(), inc=sv)
            ev_hm_b = E("vector", lambda e, ic=ic, hm2=hm2: e.tensor_reduce(
                hm2, c_hpre[:, ic * T + T // 2:(ic + 1) * T], axis=AX, op=ALU.max,
                apply_absolute_value=True), waits=(ev_stt,), inc=sv)
            ev_mh = E("vector", lambda e, hm=hm: e.tensor_tensor(
                mha, mha, hm, ALU.max), waits=(ev_hm_a, ev_mh), inc=sv)
            ev_mh = E("vector", lambda e, hm2=hm2: e.tensor_tensor(
                mha, mha, hm2, ALU.max), waits=(ev_hm_b, ev_mh), inc=sv)
        evB = {"silu_ev": ev_silu, "stt_ev": ev_stt}
    wm_st_B = wm(st)
    wm_ss_B = wm(ss)

    ev_r3mm = E("tensor", lambda e: e.matmul(
        c_ps[0:1, 4 * 512: 4 * 512 + 128], mha, c_identf[:], start=True, stop=True),
        waits=(ev_mh, evB["stt_ev"]), inc=st)
    gmh1 = c_s1[:, 1:2]
    ev_ar3row = E("vector", lambda e: e.tensor_reduce(
        gmh1, c_ps[0:1, 4 * 512: 4 * 512 + 128], axis=AX, op=ALU.max),
        waits=(ev_r3mm,), inc=sv)
    ev_pre3 = E("vector", lambda e: e.tensor_scalar(
        gmh1, gmh1, c_cols[0:1, COL["sigdb"]:COL["sigdb"] + 1], None, ALU.mult),
        waits=(ev_ar3row, ev_sig_bc), inc=sv)
    ev_c3w = E("gpsimd", lambda e: e.dma_start(out=d_cc3i[None, :], in_=gmh1),
               waits=(ev_pre3, wm(dbx)), inc=dbx, n=16)
    ev_ar3 = E("gpsimd", lambda e: e.collective_compute(
        "AllReduce", ALU.max, replica_groups=RG,
        ins=[d_cc3i.ap().opt()], outs=[d_cc3o.ap().opt()]), waits=(ev_c3w,), inc=scc)
    rshb = col("rshb")
    gmhb = col("gmhb")
    shb = gmhb
    ev_gmhb = E("gpsimd", lambda e: e.dma_start(
        out=gmhb, in_=bass.AP(d_cc3o, 0, [[0, 128], [1, 1]])),
        waits=(ev_ar3,), inc=dbh, n=16)
    ev_rshv = E("vector", lambda e: e.reciprocal(rshb, gmhb), waits=(ev_gmhb,), inc=sv)
    ev_shb = ev_rshb = ev_rshv

    c_bo2 = sb("bo2", [128, D], F32, 94 * KB + 20 * D)
    ev_bo2 = E("sync", lambda e: e.dma_start(out=c_bo2[:], in_=d_bo[:, :]),
               waits=(wm_st_B, wm(dsc)), inc=dsc, n=16)

    segb = [0, NT // 16, NT // 8, NT // 4, NT // 2, (3 * NT) // 4, NT]
    segb = sorted(set(segb))
    NSEG = len(segb) - 1
    seg_of_tile = [0] * NT
    for si in range(NSEG):
        for tt in range(segb[si], segb[si + 1]):
            seg_of_tile[tt] = si
    ev_ih = [[None] * NI for _ in range(NSEG)]
    _cv = {"tq_free": [None, None], "first": True}

    def emit_conv(si, ic):
        lo = segb[si] * 128
        w = (segb[si + 1] - segb[si]) * 128
        par = (si * NI + ic) % 2
        tqs = c_tq[:, par * 512: par * 512 + w]
        w_p1 = [ev_rshb]
        if _cv["tq_free"][par] is not None:
            w_p1.append(_cv["tq_free"][par])
        else:
            w_p1 += [wm_dsy0, wm_dsy1, evB["stt_ev"], wm_ss_B]
        ev_p1 = E("scalar", lambda e, ic=ic, lo=lo, w=w, tqs=tqs: e.activation(
            tqs, c_hpre[:, ic * T + lo: ic * T + lo + w],
            AF.Copy, bias=MAGIC, scale=rshb), waits=tuple(w_p1), inc=ss)
        w_p2 = [ev_p1]
        if _cv["first"]:
            w_p2.append(ev_mh)
            _cv["first"] = False
        ev_p2 = E("vector", lambda e, tqs=tqs: e.tensor_scalar(
            tqs, tqs, MAGIC, -QMAX - 1.0, ALU.subtract, ALU.max),
            waits=tuple(w_p2), inc=sv)
        ev_p3 = E("vector", lambda e, ic=ic, lo=lo, w=w, tqs=tqs: e.tensor_scalar(
            c_hpre[:, ic * T + lo: ic * T + lo + w], tqs,
            QMAX, None, ALU.min), waits=(ev_p2,), inc=sv)
        _cv["tq_free"][par] = ev_p3
        ev_ih[si][ic] = ev_p3

    conv_jobs = [(si, ic) for si in range(NSEG) for ic in range(NI)]
    conv_ptr = 0

    def conv_upto(n):
        nonlocal_dummy = None
        return n

    ev_dyst = [None, None]
    ev_yrd = [None, None]
    while conv_ptr < 2 * NI:
        emit_conv(*conv_jobs[conv_ptr])
        conv_ptr += 1
    for tt in range(NT):
        par = tt % 2
        half = seg_of_tile[tt]
        b0 = 0 if par == 0 else 2
        if half + 1 < NSEG:
            segw = segb[half + 1] - segb[half]
            frac = (tt - segb[half] + 1) / segw
            target = min((half + 2) * NI, (half + 1) * NI + int(NI * frac + 0.999))
        else:
            target = NSEG * NI
        target = max(target, (half + 1) * NI)
        while conv_ptr < target:
            emit_conv(*conv_jobs[conv_ptr])
            conv_ptr += 1
        w_rb = [wm_dsy0, wm_dsy1, wm_st_B]
        if ev_yrd[par] is not None:
            w_rb.append(ev_yrd[par])
        ev_acc = E("sync", lambda e, tt=tt, par=par: e.dma_start(
            out=c_accrb[:, par * D:(par + 1) * D], in_=d_x[tt * 128:(tt + 1) * 128, :]),
            waits=tuple(w_rb), inc=dsr[par], n=16)
        ev_yar = E("sync", lambda e, tt=tt, par=par: e.dma_start(
            out=c_yab[:, par * D:(par + 1) * D], in_=d_ya[tt * 128:(tt + 1) * 128, :]),
            waits=tuple(w_rb), inc=dsx[par], n=16)
        ev_ya1 = E("gpsimd", lambda e, par=par: e.tensor_tensor(
            c_accrb[:, par * D:(par + 1) * D], c_accrb[:, par * D:(par + 1) * D],
            c_bo2[:], ALU.add), waits=(ev_acc, ev_bo2), inc=sgp)
        ev_yadd = E("gpsimd", lambda e, par=par: e.tensor_tensor(
            c_accrb[:, par * D:(par + 1) * D], c_accrb[:, par * D:(par + 1) * D],
            c_yab[:, par * D:(par + 1) * D], ALU.add),
            waits=(ev_ya1, ev_yar), inc=sgp)
        ev_dm = None
        for ic in range(NI):
            for h in range(NVH):
                lo, hi = h * 512, min(D, (h + 1) * 512)
                w_d = []
                if h == 0:
                    w_d = [ev_ih[half][ic]]
                if ic == 0 and h == 0:
                    w_d.append(ev_wd_done)
                    w_d.append(ev_yrd[par] if ev_yrd[par] is not None else wm_ss_B)
                ev_dm = E("tensor", lambda e, tt=tt, ic=ic, lo=lo, hi=hi, b0=b0: e.matmul(
                    c_ps[:, b0 * 512 + lo: b0 * 512 + hi],
                    c_hpre[:, ic * T + tt * 128: ic * T + tt * 128 + 128],
                    c_wdqT[:, ic * D + lo: ic * D + hi],
                    start=(ic == 0), stop=(ic == NI - 1)),
                    waits=tuple(w_d),
                    inc=(st if (ic == NI - 1 and h == NVH - 1) else None))
        w_ev = [ev_dm, ev_yadd, ev_shb]
        if ev_yrd[par] is not None:
            w_ev.append(ev_yrd[par])
        if ev_dyst[par] is not None:
            w_ev.append(ev_dyst[par])
        ev_ye = E("vector", lambda e, par=par, b0=b0: e.scalar_tensor_tensor(
            c_ytile[:, par * D:(par + 1) * D], c_ps[:, b0 * 512: b0 * 512 + D], shb,
            c_accrb[:, par * D:(par + 1) * D], ALU.mult, ALU.add), waits=tuple(w_ev), inc=sv)
        ev_yrd[par] = ev_ye
        ev_dyst[par] = E("sync", lambda e, tt=tt, par=par: e.dma_start(
            out=d_y[tt * 128:(tt + 1) * 128, :], in_=c_ytile[:, par * D:(par + 1) * D]),
            waits=(ev_ye,), inc=dsy2[par], n=16)

    for semx in (dsc, dsx0, dsx1, dsx2, dsx3, dsw0, dsw1, dsp, dsya, dsyb,
                 dwr, dswg0, dswg1, dswu0, dswu1, dsra, dsrb, dsy2a, dsy2b,
                 dt, dbx, dbh, dbs):
        E("sync", lambda e: None, waits=(wm(semx),))
    E("gpsimd", lambda e: None, waits=(wm(dgv), wm(dgk), wm(dgq), wm(dgo), wm(scc)))

    with stack:
        with contextlib.ExitStack() as semctx:
            for s in sems:
                s.h = semctx.enter_context(nc.semaphore(s.name))
            with nc.Block() as block:
                @block.tensor
                def _(eng):
                    for f in queues["tensor"]:
                        f(eng)

                @block.vector
                def _(eng):
                    for f in queues["vector"]:
                        f(eng)

                @block.scalar
                def _(eng):
                    for f in queues["scalar"]:
                        f(eng)

                @block.sync
                def _(eng):
                    for f in queues["sync"]:
                        f(eng)

                @block.gpsimd
                def _(eng):
                    for f in queues["gpsimd"]:
                        f(eng)
    return nc


def _shard_inputs(cfg, L0_attn_x, L0_attn_Wq, L0_attn_Wk, L0_attn_Wv, L0_attn_Wo,
                  L0_attn_bo, L0_attn_Wf, L0_attn_bf, L0_ffn_clip_x, L0_ffn_clip_down,
                  L0_ffn_w_gate, L0_ffn_w_up, L0_ffn_w_down):
    T, D = cfg.T, cfg.D
    f32 = np.float32
    x = np.asarray(L0_attn_x, f32).reshape(-1, D)
    wf = np.asarray(L0_attn_Wf, f32)
    fd = wf.shape[0]
    wfbd = np.zeros((128, 128), f32)
    wfbd[0:fd, 0:fd] = wf
    wfbd[fd:2 * fd, fd:2 * fd] = wf
    bf = np.asarray(L0_attn_bf, f32)
    bfr = np.concatenate([bf, bf]).reshape(128, 1).astype(f32)
    bor = np.ascontiguousarray(np.broadcast_to(np.asarray(L0_attn_bo, f32), (128, D)))
    mask = np.zeros((128, 128), f32)
    mask[0:fd, 0:fd] = 1.0
    mask[fd:2 * fd, fd:2 * fd] = 1.0
    common = {
        "wq": np.ascontiguousarray(np.asarray(L0_attn_Wq, f32)),
        "wk": np.ascontiguousarray(np.asarray(L0_attn_Wk, f32)),
        "wv": np.ascontiguousarray(np.asarray(L0_attn_Wv, f32)),
        "wo": np.ascontiguousarray(np.asarray(L0_attn_Wo, f32)),
        "wfbd": wfbd, "bfr": bfr, "bor": bor, "maskbd": mask,
        "ident": np.eye(128, dtype=ml_dtypes.bfloat16),
        "identf": np.eye(128, dtype=np.float32),
        "onescol": np.ones((128, 1), dtype=ml_dtypes.bfloat16),
        "clip2": np.array([np.asarray(L0_ffn_clip_x, f32).ravel()[0],
                           np.asarray(L0_ffn_clip_down, f32).ravel()[0]], f32),
        "wg": np.ascontiguousarray(np.asarray(L0_ffn_w_gate, f32)),
        "wu": np.ascontiguousarray(np.asarray(L0_ffn_w_up, f32)),
        "wd": np.ascontiguousarray(np.asarray(L0_ffn_w_down, f32)),
    }
    in_maps = []
    for c in range(cfg.n_cores):
        m = dict(common)
        m["x"] = np.ascontiguousarray(x[c * T:(c + 1) * T, :])
        in_maps.append(m)
    return in_maps


def kernel(L0_attn_x, L0_attn_Wq, L0_attn_Wk, L0_attn_Wv, L0_attn_Wo, L0_attn_bo,
           L0_attn_Wf, L0_attn_bf, L0_ffn_clip_x, L0_ffn_clip_down, L0_ffn_w_gate,
           L0_ffn_w_up, L0_ffn_w_down):
    cfg = REAL
    in_maps = _shard_inputs(cfg, L0_attn_x, L0_attn_Wq, L0_attn_Wk, L0_attn_Wv,
                            L0_attn_Wo, L0_attn_bo, L0_attn_Wf, L0_attn_bf,
                            L0_ffn_clip_x, L0_ffn_clip_down, L0_ffn_w_gate,
                            L0_ffn_w_up, L0_ffn_w_down)
    nc = build_program(cfg)
    res = run_bass_kernel_spmd(nc, in_maps, core_ids=list(range(cfg.n_cores)))
    y = np.concatenate([np.asarray(r["y"], np.float32) for r in res.results], axis=0)
    return y.reshape(1, cfg.n_cores * cfg.T, cfg.D)
